# version 08
# baseline (speedup 1.0000x reference)
import contextlib
import math

import numpy as np
import ml_dtypes
import concourse.bass as bass
import concourse.mybir as mybir
from concourse.bass_utils import run_bass_kernel_spmd

F32 = mybir.dt.float32
BF16 = mybir.dt.bfloat16
I32 = mybir.dt.int32
AF = mybir.ActivationFunctionType
ALU = mybir.AluOpType
AX = mybir.AxisListType

KB = 1024
ISQ = 1.0 / math.sqrt(128.0)
ESH = -12.0
PI = math.pi
I2PI = 1.0 / (2.0 * PI)
LN10K = math.log(10000.0)

OPS = None


def em(q, waits, fn, incs=()):
    def run(eng, waits=tuple(waits), fn=fn, incs=tuple(incs)):
        for h, v in waits:
            if v > 0:
                eng.wait_ge(h, v)
        ins = fn(eng)
        for h, k in incs:
            ins = ins.then_inc(h, k)
    OPS[q].append(run)


class Ctr:
    def __init__(self, es, nc, name):
        self.h = es.enter_context(nc.semaphore(name))
        self.n = 0

    def bump(self, k=1):
        self.n += k
        return (self.h, k)

    def at(self, v):
        return (self.h, v)

    def now(self):
        return (self.h, self.n)


class PCtr:
    def __init__(self, es, nc, name):
        self.c = [Ctr(es, nc, name + "0"), Ctr(es, nc, name + "1")]

    def __getitem__(self, i):
        return self.c[i % 2]

    def both_at(self, v):
        return [self.c[0].at(v), self.c[1].at(v)]


def build():
    global OPS
    OPS = {"t": [], "v": [], "s": [], "y": [], "g": []}
    nc = bass.Bass("TRN2", target_bir_lowering=False)

    dram = lambda name, shape, dt, kind: nc.dram_tensor(name, shape, dt, kind=kind).ap()
    xT = dram("xT", [4096, 8192], F32, "ExternalInput")
    xT_own = dram("xT_own", [4096, 1024], F32, "ExternalInput")
    x_own = dram("x_own", [1024, 4096], F32, "ExternalInput")
    qkv_wT = dram("qkv_wT", [4096, 4608], F32, "ExternalInput")
    dense_wT = dram("dense_wT", [4096, 4096], F32, "ExternalInput")
    aw_T = dram("aw_T", [4096, 68], F32, "ExternalInput")
    bTd = dram("bT", [64, 4096], F32, "ExternalInput")
    qkv_b = dram("qkv_b", [1, 4608], F32, "ExternalInput")
    ident_d = dram("ident", [128, 128], BF16, "ExternalInput")
    selq_d = dram("selq", [128, 8, 128], BF16, "ExternalInput")
    masks_d = dram("masks", [128, 8, 128], BF16, "ExternalInput")
    posq_d = dram("posq", [128, 8], F32, "ExternalInput")
    poskv_d = dram("poskv", [128, 32], F32, "ExternalInput")
    y = dram("y", [1024, 4096], F32, "ExternalOutput")
    q_d = dram("q_d", [1024, 4096], BF16, "Internal")
    k_d = dram("k_d", [8192, 256], BF16, "Internal")
    wm2_d = dram("wm2_d", [1024, 128], BF16, "Internal")

    es = contextlib.ExitStack()
    with es:
        sb = lambda name, shape, dt, off: nc.alloc_sbuf_tensor_at(name, shape, dt, offset=off).ap()
        BASE = 16512
        A, C, B, E = BASE, BASE + 65536, BASE + 98304, BASE + 206848
        xnT_blk = sb("xnT_blk", [128, 2, 32, 512], BF16, A)
        xnT_own = sb("xnT_own", [128, 32, 1024], BF16, A)
        qT = sb("qT", [128, 32, 1024], BF16, A)
        sg = sb("sg", [128, 8, 4096], BF16, A)
        v_cache = sb("v_cache", [128, 64, 2, 128], BF16, C)
        bT_sb = sb("bT_sb", [64, 4096], BF16, C)
        xf = sb("xf", [128, 2, 4, 512], F32, B)
        sqb = sb("sqb", [128, 8, 512], BF16, B + 87 * KB)
        cskv = sb("cskv", [128, 512], F32, B + 18 * KB)
        kvt1 = sb("kvt1", [128, 512], F32, B + 20 * KB)
        mrow = sb("mrow", [1, 2, 512], F32, B + 22 * KB)
        nrrow = sb("nrrow", [1, 2, 512], F32, B + 26 * KB)
        mrow_bf = sb("mrow_bf", [1, 2, 512], BF16, B + 30 * KB)
        wkv = sb("wkv", [128, 32, 512], BF16, B + 32 * KB)
        kvtab_c = sb("kvtab_c", [128, 32, 32], BF16, B + 64 * KB)
        kvtab_s = sb("kvtab_s", [128, 32, 32], BF16, B + 66 * KB)
        kvev = sb("kvev", [128, 2, 512], BF16, B + 68 * KB)
        bias_kv = sb("bias_kv", [128, 512], F32, B + 70 * KB)
        b_sb = sb("b_sb", [1, 4608], BF16, B + 72 * KB)
        ropes_kv = sb("ropes_kv", [128, 4, 128], BF16, B + 81 * KB)
        nrrow_bf = sb("nrrow_bf", [1, 2, 512], BF16, B + 82 * KB)
        m_all = sb("m_all", [128, 64], F32, B + 84 * KB)
        nr_all = sb("nr_all", [128, 64], F32, B + 84 * KB + 256)
        m_allb = sb("m_allb", [128, 64], BF16, B + 84 * KB + 512)
        nr_allb = sb("nr_allb", [128, 64], BF16, B + 84 * KB + 640)
        selq = sb("selq", [128, 8, 128], BF16, B + 85 * KB)
        wq = sb("wq", [128, 2, 32, 512], BF16, B + 32 * KB)
        csq_sb = sb("csq_sb", [128, 2, 512], F32, B + 16 * KB)
        csA_sb = sb("csA_sb", [128, 68], F32, B + 22 * KB)
        ropes_q = sb("ropes_q", [128, 4, 128], BF16, B + 22 * KB + 512)
        awT_sb = sb("awT_sb", [128, 32, 68], BF16, B + 23 * KB + 512)
        bias_q = sb("bias_q", [128, 4096], BF16, B + 96 * KB)
        qev = sb("qev", [128, 2, 512], BF16, B + 104 * KB)
        ctxT = sb("ctxT", [128, 32, 8, 128], BF16, B)
        kT = sb("kT", [128, 2, 8192], BF16, B + 64 * KB)
        probs = sb("probs", [128, 3, 1024], BF16, B + 96 * KB)
        rbc = sb("rbc", [128, 1024], F32, B + 102 * KB)
        wd = sb("wd", [128, 2, 32, 256], BF16, B + 64 * KB)
        wmT_sb = sb("wmT_sb", [128, 1024], BF16, B + 96 * KB)
        xres = sb("xres", [128, 4096], F32, B)
        outb = sb("outb", [128, 2, 512], F32, B + 64 * KB)
        sgf = sb("sgf", [128, 512], F32, B + 68 * KB)
        masks = sb("masks_sb", [128, 8, 128], BF16, E)
        qtab_c = sb("qtab_c", [128, 8, 32], BF16, E + 2048)
        qtab_s = sb("qtab_s", [128, 8, 32], BF16, E + 2560)
        ident = sb("ident_sb", [128, 128], BF16, E + 3072)
        ones = sb("ones_sb", [128, 128], BF16, E + 3328)
        consts = sb("consts", [128, 8], F32, E + 3584)
        invfb = sb("invfb", [128, 32], F32, E + 3616)
        tabsc = sb("tabsc", [128, 4, 32], F32, E + 3744)
        tabsc_i = sb("tabsc_i", [128, 32], I32, E + 4256)
        posq = sb("posq_sb", [128, 8], F32, E + 4384)
        poskv = sb("poskv_sb", [128, 32], F32, E + 4416)
        stat = sb("stat", [128, 8], F32, E + 4544)
        mid_f = sb("mid_f", [128, 80], F32, E + 4576)
        gates = sb("gates", [128, 8, 4], F32, E + 4896)
        wmsc = sb("wmsc", [128, 128], BF16, E + 5024)
        m_own = sb("m_own", [128, 8], F32, E + 5280)
        nr_own = sb("nr_own", [128, 8], F32, E + 5312)
        gsc = sb("gsc", [128, 256], BF16, E + 5408)

        PS = nc.alloc_psum_tensor("PS", [128, 8, 512], F32).ap()

        T = Ctr(es, nc, "T")
        V = Ctr(es, nc, "V")
        S = Ctr(es, nc, "S")
        G = Ctr(es, nc, "G")
        dmisc = Ctr(es, nc, "dmisc")
        dwA = Ctr(es, nc, "dwA")
        dwq = PCtr(es, nc, "dwq")
        dwd = PCtr(es, nc, "dwd")
        dbt = Ctr(es, nc, "dbt")
        dwmT = Ctr(es, nc, "dwmT")
        dxf = PCtr(es, nc, "dxf")
        dxo = PCtr(es, nc, "dxo")
        dkd = PCtr(es, nc, "dkd")
        dqd = PCtr(es, nc, "dqd")
        dqT = Ctr(es, nc, "dqT")
        dkT = Ctr(es, nc, "dkT")
        dxr = Ctr(es, nc, "dxr")
        dwm = Ctr(es, nc, "dwm")
        dy = PCtr(es, nc, "dy")

        em("g", [], lambda g: g.dma_start(out=ident[:], in_=ident_d[:]), [dmisc.bump(16)])
        em("g", [], lambda g: g.dma_start(out=masks[:], in_=masks_d[:]), [dmisc.bump(16)])
        em("g", [], lambda g: g.dma_start(out=posq[:], in_=posq_d[:]), [dmisc.bump(16)])
        em("g", [], lambda g: g.dma_start(out=poskv[:], in_=poskv_d[:]), [dmisc.bump(16)])
        em("g", [], lambda g: g.dma_start(out=selq[:], in_=selq_d[:]), [dmisc.bump(16)])
        em("g", [], lambda g: g.dma_start(out=b_sb[:], in_=qkv_b[:]), [dwA.bump(16)])
        em("g", [], lambda g: g.dma_start(
            out=wkv[:], in_=qkv_wT[:, 4096:4608].rearrange("(fc p) o -> p fc o", p=128)),
            [dwA.bump(16)])
        em("g", [], lambda g: g.iota(tabsc[:, 0, :], pattern=[[1, 32]], base=0,
                                     channel_multiplier=0,
                                     allow_small_or_imprecise_dtypes=True), [G.bump()])

        em("v", [], lambda v: v.memset(wmsc[:], 0.0))
        em("v", [], lambda v: v.memset(consts[:, 0:1], ESH))
        em("v", [], lambda v: v.memset(consts[:, 1:2], PI / 2.0))
        em("v", [], lambda v: v.memset(consts[:, 2:3], 1e-5))
        em("v", [], lambda v: v.memset(ones[:], 1.0), [V.bump()])
        v_ones = V.n
        em("v", [G.at(1)], lambda v: v.tensor_scalar(
            out=tabsc[:, 1, :], in0=tabsc[:, 0, :], scalar1=-LN10K / 32.0, scalar2=None,
            op0=ALU.mult), [V.bump()])
        em("s", [V.now()], lambda s: s.activation(out=invfb[:], in_=tabsc[:, 1, :],
                                                  func=AF.Exp), [S.bump()])
        s_invf = S.n

        def table_ops(pos_ap, dc, ds):
            em("v", [S.at(s_invf), dmisc.at(80), S.now()], lambda v, p=pos_ap:
               v.tensor_scalar(out=tabsc[:, 1, :], in0=invfb[:], scalar1=p,
                               scalar2=None, op0=ALU.mult), [V.bump()])
            for off, dst in ((None, ds), (0.25, dc)):
                if off is None:
                    em("v", [V.now(), S.now()], lambda v: v.tensor_scalar(
                        out=tabsc[:, 2, :], in0=tabsc[:, 1, :], scalar1=I2PI,
                        scalar2=None, op0=ALU.mult), [V.bump()])
                else:
                    em("v", [V.now(), S.now()], lambda v, off=off: v.tensor_scalar(
                        out=tabsc[:, 2, :], in0=tabsc[:, 1, :], scalar1=I2PI,
                        scalar2=off, op0=ALU.mult, op1=ALU.add), [V.bump()])
                em("v", [V.now()], lambda v: v.tensor_copy(out=tabsc_i[:], in_=tabsc[:, 2, :]),
                   [V.bump()])
                em("v", [V.now()], lambda v: v.tensor_copy(out=tabsc[:, 3, :], in_=tabsc_i[:]),
                   [V.bump()])
                em("v", [V.now()], lambda v: v.tensor_sub(
                    out=tabsc[:, 2, :], in0=tabsc[:, 2, :], in1=tabsc[:, 3, :]), [V.bump()])
                em("v", [V.now()], lambda v: v.tensor_scalar(
                    out=tabsc[:, 3, :], in0=tabsc[:, 2, :], scalar1=0.5, scalar2=None,
                    op0=ALU.is_ge), [V.bump()])
                em("v", [V.now()], lambda v: v.tensor_sub(
                    out=tabsc[:, 2, :], in0=tabsc[:, 2, :], in1=tabsc[:, 3, :]), [V.bump()])
                em("v", [V.now()], lambda v: v.tensor_scalar(
                    out=tabsc[:, 2, :], in0=tabsc[:, 2, :], scalar1=2.0 * PI, scalar2=None,
                    op0=ALU.mult), [V.bump()])
                em("s", [V.now()], lambda s, d=dst: s.activation(
                    out=d, in_=tabsc[:, 2, :], func=AF.Sin), [S.bump()])

        for qt in range(8):
            table_ops(posq[:, qt:qt + 1], qtab_c[:, qt, :], qtab_s[:, qt, :])
        for kt in range(32):
            table_ops(poskv[:, kt:kt + 1], kvtab_c[:, kt, :], kvtab_s[:, kt, :])
        s_tabs = S.n

        em("t", [dwA.at(32), V.at(v_ones)], lambda t: t.matmul(
            PS[:, 0, :], ones[0:1, :], b_sb[0:1, 4096:4608], start=True, stop=True),
            [T.bump()])
        t_bkv = T.n
        em("v", [T.at(t_bkv)], lambda v: v.tensor_copy(out=bias_kv[:], in_=PS[:, 0, :]),
           [V.bump()])
        v_biaskv = V.n
        for f in range(32):
            em("t", [T.at(t_bkv)] if f == 0 else [], lambda t, f=f: t.matmul(
                PS[:, 7, :], ones[:], wkv[:, f, :], start=(f == 0), stop=(f == 31)),
                [T.bump()] if f == 31 else ())
        em("v", [T.now()], lambda v: v.tensor_copy(out=cskv[:], in_=PS[:, 7, :]),
           [V.bump()])
        v_cskv = V.n

        def rope(dst3, tabc, tabs, nh, wfirst, scratch):
            ev = dst3[:, :, 0:64:2]
            od = dst3[:, :, 1:64:2]
            cb = tabc.unsqueeze(1).broadcast_to([128, nh, 32])
            sbt = tabs.unsqueeze(1).broadcast_to([128, nh, 32])
            rr = [scratch[:, k, 0:32 * nh].rearrange("p (h i) -> p h i", h=nh)
                  for k in range(4)]
            em("v", wfirst, lambda v: v.tensor_mul(out=rr[0], in0=ev, in1=cb))
            em("v", [], lambda v: v.tensor_mul(out=rr[1], in0=od, in1=sbt))
            em("v", [], lambda v: v.tensor_mul(out=rr[2], in0=od, in1=cb))
            em("v", [], lambda v: v.tensor_mul(out=rr[3], in0=ev, in1=sbt), [V.bump()])
            em("v", [V.now()], lambda v: v.tensor_sub(out=ev, in0=rr[0], in1=rr[1]))
            em("v", [], lambda v: v.tensor_add(out=od, in0=rr[2], in1=rr[3]), [V.bump()])

        ckv, ckve, ckvr = [], [], []
        ckv_last = []
        ccast = []
        cast_l = []
        csq_last = []
        crowbf = []
        cevac = {}
        crow_read = []
        tsq = {}
        tmean_stop = {}
        cown = {}

        def emit_stats_tc(tc):
            par = tc % 2
            cc0 = {}
            for l2 in range(8):
                li = 8 * tc + l2
                wl = []
                if li >= 2:
                    wl.append(V.at(cast_l[li - 2]))
                em("y" if l2 % 2 == 0 else "s", wl,
                   lambda q, tc=tc, l2=l2, li=li: q.dma_start(
                       out=xf[:, li % 2, :, :],
                       in_=xT[512 * l2:512 * (l2 + 1),
                              512 * tc:512 * (tc + 1)].rearrange(
                           "(f p) t -> p f t", p=128)), [dxf[li].bump(16)])
                d_l = dxf[li].n
                for j in range(4):
                    fc = 4 * l2 + j
                    wv = []
                    if j == 0:
                        wv.append(dxf[li].at(d_l))
                        if l2 == 0 and tc >= 2:
                            wv += [T.at(ckv_last[tc - 2]), S.at(csq_last[tc - 2])]
                    em("v", wv, lambda v, tc=tc, li=li, j=j, fc=fc: v.tensor_copy(
                        out=xnT_blk[:, tc % 2, fc, :], in_=xf[:, li % 2, j, :]),
                        [V.bump()])
                    cc0[fc] = V.n
                    wt = [V.at(cc0[fc])]
                    if fc == 0 and tc >= 1:
                        wt.append(V.at(crow_read[tc - 1]))
                    em("t", wt, lambda t, tc=tc, fc=fc: t.matmul(
                        PS[:, 3, :], ones[:], xnT_blk[:, tc % 2, fc, :],
                        start=(fc == 0), stop=(fc == 31)),
                        [T.bump()] if fc == 31 else ())
                    if fc == 31:
                        tmean_stop[tc] = T.n
                    ws = [V.at(cc0[fc])]
                    if (tc, fc - 8) in tsq:
                        ws.append(T.at(tsq[(tc, fc - 8)]))
                    elif tc >= 1 and fc < 8:
                        ws.append(T.at(tsq[(tc - 1, 24 + fc)]))
                    em("s", ws, lambda s, tc=tc, fc=fc: s.activation(
                        out=sqb[:, fc % 8, :], in_=xnT_blk[:, tc % 2, fc, :],
                        func=AF.Square), [S.bump()])
                    s_here = S.n
                    wq4 = [S.at(s_here)]
                    if fc == 0 and tc >= 1:
                        wq4.append(V.at(crow_read[tc - 1]))
                    em("t", wq4, lambda t, tc=tc, fc=fc: t.matmul(
                        PS[:, 4, :], ones[:], sqb[:, fc % 8, :],
                        start=(fc == 0), stop=(fc == 31)), [T.bump()])
                    tsq[(tc, fc)] = T.n
                cast_l.append(V.n)
            ccast.append(V.n)
            csq_last.append(S.n)
            wv = [T.at(tmean_stop[tc])]
            if tc >= 2:
                wv.append(V.at(crow_read[tc - 2]))
            em("v", wv, lambda v, par=par: v.tensor_scalar(
                out=mrow[0:1, par, :], in0=PS[0:1, 3, :], scalar1=1.0 / 4096.0,
                scalar2=None, op0=ALU.mult), [V.bump()])
            em("v", [V.now()], lambda v, par=par: v.tensor_mul(
                out=nrrow[0:1, par, :], in0=mrow[0:1, par, :], in1=mrow[0:1, par, :]),
                [V.bump()])
            em("v", [V.now(), T.at(tsq[(tc, 31)])], lambda v, par=par:
               v.scalar_tensor_tensor(
                   out=nrrow[0:1, par, :], in0=PS[0:1, 4, :], scalar=1.0 / 4096.0,
                   in1=nrrow[0:1, par, :], op0=ALU.mult, op1=ALU.subtract), [V.bump()])
            em("s", [V.now()], lambda s, par=par: s.activation(
                out=nrrow[0:1, par, :], in_=nrrow[0:1, par, :], func=AF.Ln,
                bias=consts[0:1, 2:3]), [S.bump()])
            em("s", [S.now()], lambda s, par=par: s.activation(
                out=nrrow[0:1, par, :], in_=nrrow[0:1, par, :], func=AF.Exp,
                scale=-0.5), [S.bump()])
            em("v", [S.now()], lambda v, par=par: v.tensor_scalar(
                out=nrrow[0:1, par, :], in0=nrrow[0:1, par, :], scalar1=-1.0,
                scalar2=None, op0=ALU.mult), [V.bump()])
            em("v", [V.now()], lambda v, par=par: v.tensor_copy(
                out=mrow_bf[0:1, par, :], in_=mrow[0:1, par, :]), [V.bump()])
            em("v", [V.now()], lambda v, par=par: v.tensor_copy(
                out=nrrow_bf[0:1, par, :], in_=nrrow[0:1, par, :]), [V.bump()])
            crowbf.append(V.n)
            for kl in range(4):
                kt = 4 * tc + kl
                wtm = [V.at(crowbf[tc])]
                if kt >= 1:
                    wtm.append(V.at(cevac[kt - 1]))
                if kl == 0 and tc >= 2 and tc % 2 == 0:
                    wtm.append(V.at(cown[((tc - 2) // 2, 6)]))
                em("t", wtm, lambda t, par=par, kl=kl: t.matmul(
                    PS[:, 5, 0:1], mrow_bf[0:1, par, 128 * kl:128 * (kl + 1)],
                    ones[0:1, 0:1], start=True, stop=True), [T.bump()])
                em("t", [], lambda t, par=par, kl=kl: t.matmul(
                    PS[:, 6, 0:1], nrrow_bf[0:1, par, 128 * kl:128 * (kl + 1)],
                    ones[0:1, 0:1], start=True, stop=True), [T.bump()])
                tcol = T.n
                em("v", [T.at(tcol)], lambda v, kt=kt: v.tensor_copy(
                    out=m_all[:, kt:kt + 1], in_=PS[:, 5, 0:1]))
                em("v", [], lambda v, kt=kt: v.tensor_copy(
                    out=m_allb[:, kt:kt + 1], in_=PS[:, 5, 0:1]))
                em("v", [], lambda v, kt=kt: v.tensor_copy(
                    out=nr_all[:, kt:kt + 1], in_=PS[:, 6, 0:1]))
                em("v", [], lambda v, kt=kt: v.tensor_copy(
                    out=nr_allb[:, kt:kt + 1], in_=PS[:, 6, 0:1]), [V.bump()])
                cevac[kt] = V.n
            if tc % 2 == 1:
                t = (tc - 1) // 2
                for bank, src, dst in ((5, m_allb, m_own), (6, nr_allb, nr_own)):
                    for q in range(8):
                        kt = 8 * t + q
                        w = [V.at(cevac[4 * tc + 3]), dmisc.at(80)] if q == 0 else []
                        em("t", w, lambda tt, bank=bank, q=q, kt=kt, src=src: tt.matmul(
                            PS[:, bank, 0:1], selq[:, q, :], src[:, kt:kt + 1],
                            start=(q == 0), stop=(q == 7)),
                            [T.bump()] if q == 7 else ())
                    em("v", [T.now()], lambda v, bank=bank, t=t, dst=dst:
                       v.tensor_copy(out=dst[:, t:t + 1], in_=PS[:, bank, 0:1]),
                       [V.bump()])
                    cown[(t, bank)] = V.n
            crow_read.append(V.n)
            ckv_last.append(0)

        def emit_kv_tc(tc):
            par = tc % 2
            for kl in range(4):
                kt = 4 * tc + kl
                w = [V.at(ccast[tc]), dwA.at(32)]
                if kt < 3:
                    w.append(V.at(v_cskv))
                if kt >= 3:
                    w.append(V.at(ckve[kt - 3]))
                for f in range(32):
                    em("t", w if f == 0 else [],
                       lambda t, par=par, kl=kl, kt=kt, f=f: t.matmul(
                           PS[:, kt % 3, :],
                           xnT_blk[:, par, f, 128 * kl:128 * (kl + 1)],
                           wkv[:, f, :], start=(f == 0), stop=(f == 31)),
                       [T.bump()] if f == 31 else ())
                ckv.append(T.n)
                wv = [T.at(ckv[kt]), V.at(cevac[kt])]
                if kt >= 1:
                    wv.append(V.at(ckve[kt - 1]))
                if kt >= 2:
                    wv.append(dkd[kt].at(16 * (kt // 2)))
                em("v", wv, lambda v, kt=kt: v.scalar_tensor_tensor(
                    out=kvt1[:], in0=cskv[:], scalar=m_all[:, kt:kt + 1],
                    in1=PS[:, kt % 3, :], op0=ALU.mult, op1=ALU.subtract), [V.bump()])
                em("v", [V.now()], lambda v, kt=kt: v.scalar_tensor_tensor(
                    out=kvev[:, kt % 2, :], in0=kvt1[:], scalar=nr_all[:, kt:kt + 1],
                    in1=bias_kv[:], op0=ALU.mult, op1=ALU.add), [V.bump()])
                ckve.append(V.n)
                em("v", [S.at(s_tabs), V.at(ckve[kt])], lambda v, kt=kt: v.tensor_copy(
                    out=v_cache[:, kt, :, :],
                    in_=kvev[:, kt % 2, 256:512].rearrange("p (g d) -> p g d", g=2)))
                kview = kvev[:, kt % 2, 0:256].rearrange("p (h d) -> p h d", h=2)
                rope(kview, kvtab_c[:, kt % 32, :], kvtab_s[:, kt % 32, :], 2, [],
                     ropes_kv)
                ckvr.append(V.n)
                em("y", [V.at(ckvr[kt])], lambda yq, kt=kt: yq.dma_start(
                    out=k_d[128 * kt:128 * (kt + 1), :], in_=kvev[:, kt % 2, 0:256]),
                    [dkd[kt].bump(16)])
            ckv_last[tc] = T.n

        for tc in range(16):
            if tc >= 1:
                emit_kv_tc(tc - 1)
            emit_stats_tc(tc)
        emit_kv_tc(15)

        cast_o = []
        for st in range(16):
            l2, h = st // 2, st % 2
            wl = []
            if st == 0:
                wl += [T.at(ckv_last[15]), S.at(csq_last[15])]
            if st >= 2:
                wl.append(V.at(cast_o[st - 2]))
            if st < 2:
                wl.append(V.at(cast_l[126 + st]))
            em("y" if st % 2 == 0 else "s", wl,
               lambda q, l2=l2, h=h, st=st: q.dma_start(
                   out=xf[:, st % 2, :, :],
                   in_=xT_own[512 * l2:512 * (l2 + 1),
                              512 * h:512 * (h + 1)].rearrange(
                       "(f p) t -> p f t", p=128)), [dxo[st].bump(16)])
            d_l = dxo[st].n
            for j in range(4):
                fc = 4 * l2 + j
                em("v", [dxo[st].at(d_l)] if j == 0 else [],
                   lambda v, st=st, j=j, fc=fc, h=h: v.tensor_copy(
                       out=xnT_own[:, fc, 512 * h:512 * (h + 1)],
                       in_=xf[:, st % 2, j, :]), [V.bump()])
            cast_o.append(V.n)
        v_casto = V.n

        for ob in range(8):
            w = [V.at(ckve[63])] if ob == 0 else [V.at(V.n)]
            em("t", w, lambda t, ob=ob: t.matmul(PS[:, 5, :], ones[0:1, :],
                                                 b_sb[0:1, 512 * ob:512 * (ob + 1)],
                                                 start=True, stop=True), [T.bump()])
            em("v", [T.now()], lambda v, ob=ob: v.tensor_copy(
                out=bias_q[:, 512 * ob:512 * (ob + 1)], in_=PS[:, 5, :]), [V.bump()])
        v_qb = V.n
        t_qb2 = T.n

        em("g", [V.at(ckvr[63]), T.at(ckv_last[15])] + dkd.both_at(32 * 16),
           lambda g: g.dma_start(
            out=awT_sb[:], in_=aw_T.rearrange("(fc p) o -> p fc o", p=128)), [dwA.bump(16)])
        dwA_awt = dwA.n

        cqm, cqe, cqr = [], [], []
        ccsq = []
        for ob in range(8):
            wg = [T.at(ckv_last[15])]
            if ob < 2:
                wg += [T.at(t_qb2), V.at(ckvr[63])] + dkd.both_at(32 * 16)
            if ob >= 2:
                wg.append(T.at(cqm[8 * (ob - 2) + 7]))
            em("g", wg, lambda g, ob=ob: g.dma_start(
                out=wq[:, ob % 2, :, :],
                in_=qkv_wT[:, 512 * ob:512 * (ob + 1)].rearrange("(fc p) o -> p fc o", p=128)),
                [dwq[ob].bump(16)])
            d_wq = dwq[ob].n
            for f in range(32):
                w = [dwq[ob].at(d_wq)] if f == 0 else []
                if f == 0 and ob >= 1:
                    w.append(V.at(ccsq[ob - 1]))
                if f == 0 and ob == 0:
                    w.append(V.at(v_qb))
                em("t", w, lambda t, ob=ob, f=f: t.matmul(
                    PS[:, 4, :], ones[:], wq[:, ob % 2, f, :],
                    start=(f == 0), stop=(f == 31)), [T.bump()] if f == 31 else ())
            em("v", [T.now()] + ([V.at(ckve[63])] if ob == 0 else []),
               lambda v, ob=ob: v.tensor_copy(
                out=csq_sb[:, ob % 2, :], in_=PS[:, 4, :]), [V.bump()])
            ccsq.append(V.n)
            for t in range(8):
                n = 8 * ob + t
                w = [dwq[ob].at(d_wq), V.at(v_casto), V.at(ccsq[ob])]
                if t == 0:
                    w += [V.at(cown[(0, 5)]), V.at(cown[(7, 6)])]
                if n >= 4:
                    w.append(V.at(cqe[n - 4]))
                for f in range(32):
                    em("t", w if f == 0 else [],
                       lambda tt, ob=ob, t=t, f=f: tt.matmul(
                           PS[:, t % 4, :], xnT_own[:, f, 128 * t:128 * (t + 1)],
                           wq[:, ob % 2, f, :], start=(f == 0), stop=(f == 31)),
                       [T.bump()] if f == 31 else ())
                cqm.append(T.n)
                wv = [T.at(cqm[n])]
                if n == 0:
                    wv.append(V.at(ckve[63]))
                if n >= 1:
                    wv.append(V.at(cqe[n - 1]))
                if n >= 2:
                    wv.append(dqd[n].at(16 * (n // 2)))
                em("v", wv, lambda v, n=n, t=t, ob=ob: v.scalar_tensor_tensor(
                    out=kvt1[:], in0=csq_sb[:, ob % 2, :], scalar=m_own[:, t:t + 1],
                    in1=PS[:, t % 4, :], op0=ALU.mult, op1=ALU.subtract), [V.bump()])
                em("v", [V.now()], lambda v, n=n, t=t, ob=ob: v.scalar_tensor_tensor(
                    out=qev[:, n % 2, :], in0=kvt1[:], scalar=nr_own[:, t:t + 1],
                    in1=bias_q[:, 512 * ob:512 * (ob + 1)], op0=ALU.mult, op1=ALU.add),
                    [V.bump()])
                cqe.append(V.n)
                qview = qev[:, n % 2, :].rearrange("p (h d) -> p h d", h=4)
                rope(qview, qtab_c[:, t, :], qtab_s[:, t, :], 4, [V.at(cqe[n])],
                     ropes_q)
                cqr.append(V.n)
                em("y", [V.at(cqr[n])], lambda yq, n=n, t=t, ob=ob: yq.dma_start(
                    out=q_d[128 * t:128 * (t + 1), 512 * ob:512 * (ob + 1)],
                    in_=qev[:, n % 2, :]), [dqd[n].bump(16)])

        for f in range(32):
            w = [dwA.at(dwA_awt), V.at(ccsq[7])] if f == 0 else []
            em("t", w, lambda t, f=f: t.matmul(
                PS[:, 4, 0:68], ones[:], awT_sb[:, f, :],
                start=(f == 0), stop=(f == 31)), [T.bump()] if f == 31 else ())
        em("v", [T.now(), V.at(ckve[63])], lambda v: v.tensor_copy(
            out=csA_sb[:], in_=PS[:, 4, 0:68]), [V.bump()])
        v_csA = V.n
        cme = []
        t_mid_last = 0
        for t in range(8):
            w = [dwA.at(dwA_awt), V.at(cqe[63])]
            if t >= 2:
                w.append(V.at(cme[t - 2]))
            for f in range(32):
                em("t", w if f == 0 else [],
                   lambda tt, t=t, f=f: tt.matmul(
                       PS[:, t % 2, 0:68], xnT_own[:, f, 128 * t:128 * (t + 1)],
                       awT_sb[:, f, :], start=(f == 0), stop=(f == 31)),
                   [T.bump()] if f == 31 else ())
            t_mid_last = T.n
            wmf = [T.at(t_mid_last), V.at(v_csA)]
            if t >= 1:
                wmf.append(V.at(cme[t - 1]))
            em("v", wmf, lambda v, t=t: v.scalar_tensor_tensor(
                out=mid_f[:, 0:68], in0=csA_sb[:], scalar=m_own[:, t:t + 1],
                in1=PS[:, t % 2, 0:68], op0=ALU.mult, op1=ALU.subtract), [V.bump()])
            em("v", [V.now()], lambda v, t=t: v.tensor_scalar(
                out=mid_f[:, 0:68], in0=mid_f[:, 0:68], scalar1=nr_own[:, t:t + 1],
                scalar2=None, op0=ALU.mult), [V.bump()])
            L = lambda j: mid_f[:, 64 + j:65 + j]
            em("v", [V.now()], lambda v: v.tensor_max(out=stat[:, 0:1], in0=L(0), in1=L(1)))
            em("v", [], lambda v: v.tensor_max(out=stat[:, 1:2], in0=L(2), in1=L(3)))
            em("v", [], lambda v: v.tensor_tensor(out=stat[:, 2:3], in0=L(0), in1=L(1),
                                                  op=ALU.min))
            em("v", [], lambda v: v.tensor_tensor(out=stat[:, 3:4], in0=L(2), in1=L(3),
                                                  op=ALU.min), [V.bump()])
            em("v", [V.now()], lambda v: v.tensor_tensor(out=stat[:, 4:5], in0=stat[:, 0:1],
                                                         in1=stat[:, 1:2], op=ALU.min))
            em("v", [], lambda v: v.tensor_max(out=stat[:, 5:6], in0=stat[:, 2:3],
                                               in1=stat[:, 3:4]))
            em("v", [], lambda v: v.tensor_max(out=stat[:, 6:7], in0=stat[:, 0:1],
                                               in1=stat[:, 1:2]), [V.bump()])
            em("v", [V.now()], lambda v: v.tensor_max(out=stat[:, 4:5], in0=stat[:, 4:5],
                                                      in1=stat[:, 5:6]))
            em("v", [], lambda v: v.tensor_scalar(out=stat[:, 7:8], in0=stat[:, 6:7],
                                                  scalar1=-1.0, scalar2=None,
                                                  op0=ALU.mult), [V.bump()])
            em("s", [V.now()], lambda s: s.activation(
                out=mid_f[:, 72:76], in_=mid_f[:, 64:68], func=AF.Exp, bias=stat[:, 7:8]),
                [S.bump()])
            em("v", [S.now()], lambda v: v.tensor_scalar(
                out=mid_f[:, 76:80], in0=mid_f[:, 64:68], scalar1=stat[:, 4:5],
                scalar2=None, op0=ALU.is_ge), [V.bump()])
            em("v", [V.now()], lambda v: v.tensor_mul(
                out=mid_f[:, 72:76], in0=mid_f[:, 72:76], in1=mid_f[:, 76:80]), [V.bump()])
            em("v", [V.now()], lambda v: v.reduce_sum(
                out=stat[:, 0:1], in_=mid_f[:, 72:76], axis=AX.X), [V.bump()])
            em("v", [V.now()], lambda v: v.reciprocal(out=stat[:, 1:2], in_=stat[:, 0:1]),
               [V.bump()])
            em("v", [V.now()], lambda v, t=t: v.tensor_scalar(
                out=gates[:, t, :], in0=mid_f[:, 72:76], scalar1=stat[:, 1:2],
                scalar2=4.0, op0=ALU.mult, op1=ALU.mult), [V.bump()])
            for e in range(4):
                w = [V.now()]
                if e == 0 and t >= 1:
                    w.append(dwm.at(16 * t))
                em("v", w if e == 0 else [], lambda v, t=t, e=e: v.tensor_scalar(
                    out=wmsc[:, 16 * e:16 * (e + 1)], in0=mid_f[:, 16 * e:16 * (e + 1)],
                    scalar1=gates[:, t, e:e + 1], scalar2=None, op0=ALU.mult), [V.bump()])
            cme.append(V.n)
            v_wm = V.n
            em("g", [V.at(v_wm)], lambda g, t=t: g.dma_start(
                out=wm2_d[128 * t:128 * (t + 1), :], in_=wmsc[:]), [dwm.bump(16)])

        for g2 in range(2):
            for ss in range(8):
                w = (dkd.both_at(32 * 16) + [V.at(cqr[63]), T.at(t_mid_last)]
                     ) if (g2 == 0 and ss == 0) else []
                em("s", w, lambda q, g2=g2, ss=ss: q.dma_start(
                    out=kT[:, g2, 1024 * ss:1024 * (ss + 1)],
                    in_=k_d[1024 * ss:1024 * (ss + 1), 128 * g2:128 * (g2 + 1)],
                    transpose=True), [dkT.bump(16)])
        for h in range(32):
            w = dqd.both_at(32 * 16) + [T.at(t_mid_last)] if h == 0 else []
            em("s", w, lambda q, h=h: q.dma_start(
                out=qT[:, h, :], in_=q_d[:, 128 * h:128 * (h + 1)], transpose=True),
                [dqT.bump(16)])

        meta = []
        gidx = 0
        for qt in range(8):
            wwin, bb = qt % 4, qt // 4
            for g2 in range(2):
                for h2 in range(2):
                    hh0 = 16 * g2 + 8 * h2
                    nch = 8 * (wwin + 1)
                    for j in range(nch):
                        meta.append(dict(qt=qt, wwin=wwin, g2=g2, hh0=hh0, j=j,
                                         nch=nch, kt=32 * bb + j, gidx=gidx))
                    gidx += 1
        n_groups = gidx
        NGJ = len(meta)
        cex, csc, cat, cready = [], [], [], []
        cse, cve = [], []

        def emit_scores(GJ):
            m = meta[GJ]
            bs = 2 * (GJ % 2)
            w = []
            if GJ == 0:
                w += [dqT.at(32 * 16), dkT.at(16 * 16), V.at(cme[7]), S.at(s_tabs)]
            if GJ >= 2:
                w.append(S.at(cex[GJ - 2]))
            for half in range(2):
                em("t", w if half == 0 else [],
                   lambda t, g2=m["g2"], kt=m["kt"], qt=m["qt"], hh0=m["hh0"],
                   half=half, bs=bs:
                   t.matmul(PS[:, bs + half, :],
                            kT[:, g2, 128 * kt:128 * (kt + 1)],
                            qT[:, hh0 + 4 * half:hh0 + 4 * (half + 1),
                               128 * qt:128 * (qt + 1)],
                            start=True, stop=True),
                   [T.bump()] if half == 1 else ())
            csc.append(T.n)
            we = [T.at(csc[GJ])]
            if GJ == 0:
                we.append(V.at(cqe[63]))
            if GJ >= 3:
                we.append(T.at(cat[GJ - 3]))
            em("s", we, lambda s, GJ=GJ, bs=bs: s.activation(
                out=probs[:, GJ % 3, :],
                in_=PS[:, bs:bs + 2, :].rearrange("p b n -> p (b n)"),
                func=AF.Exp, scale=ISQ, bias=consts[:, 0:1]), [S.bump()])
            cex.append(S.n)
            if m["j"] >= 8 * m["wwin"]:
                wrel = m["j"] - 8 * m["wwin"]
                em("v", [S.at(cex[GJ])], lambda v, GJ=GJ, wrel=wrel:
                   v.tensor_mul(
                       out=probs[:, GJ % 3, :].rearrange("p (h q) -> p h q", h=8),
                       in0=probs[:, GJ % 3, :].rearrange("p (h q) -> p h q", h=8),
                       in1=masks[:, wrel, :].unsqueeze(1).broadcast_to(
                           [128, 8, 128])), [V.bump()])
                cready.append((V.h, V.n))
            else:
                cready.append((S.h, cex[GJ]))

        def emit_sumctx(GJ):
            m = meta[GJ]
            wsum = [cready[GJ]]
            if m["j"] == 0 and m["gidx"] >= 1:
                wsum += [S.at(cse[m["gidx"] - 1]), V.at(cve[m["gidx"] - 1])]
            first, last = (m["j"] == 0), (m["j"] == m["nch"] - 1)
            for half in range(2):
                em("t", wsum if half == 0 else [],
                   lambda t, GJ=GJ, half=half, first=first, last=last:
                   t.matmul(PS[:, 4 + half, :], ones[:],
                            probs[:, GJ % 3, 512 * half:512 * (half + 1)],
                            start=first, stop=last))
            for half in range(2):
                em("t", [], lambda t, GJ=GJ, half=half, kt=m["kt"], g2=m["g2"],
                   first=first, last=last:
                   t.matmul(PS[:, 6 + half, :], v_cache[:, kt, g2, :],
                            probs[:, GJ % 3, 512 * half:512 * (half + 1)],
                            start=first, stop=last),
                   [T.bump()] if half == 1 else ())
            cat.append(T.n)
            if last:
                g = m["gidx"]
                we = [T.at(cat[GJ])]
                if g == 0:
                    we += dqd.both_at(32 * 16)
                em("s", we, lambda s: s.activation(
                    out=rbc[:], in_=PS[:, 4:6, :].rearrange("p b n -> p (b n)"),
                    func=AF.Ln), [S.bump()])
                em("s", [S.now()], lambda s: s.activation(
                    out=rbc[:], in_=rbc[:], func=AF.Exp, scale=-1.0), [S.bump()])
                cse.append(S.n)
                em("v", [S.at(cse[g]), T.at(cat[GJ])],
                   lambda v, hh0=m["hh0"], qt=m["qt"]: v.tensor_mul(
                       out=ctxT[:, hh0:hh0 + 8, qt, :],
                       in0=PS[:, 6:8, :].rearrange("p b n -> p (b n)").rearrange(
                           "p (h q) -> p h q", h=8),
                       in1=rbc[:].rearrange("p (h q) -> p h q", h=8)), [V.bump()])
                cve.append(V.n)

        for GJ in range(NGJ + 1):
            if GJ < NGJ:
                emit_scores(GJ)
            if GJ >= 1:
                emit_sumctx(GJ - 1)
        t_att_end = T.n
        s_att_end = S.n

        em("g", [T.at(t_att_end)], lambda g: g.dma_start(
            out=bT_sb[:], in_=bTd[:]), [dbt.bump(16)])
        em("s", [dwm.at(8 * 16), T.at(t_att_end), V.at(cve[n_groups - 1])],
           lambda q: q.dma_start(out=wmT_sb[:], in_=wm2_d[:], transpose=True),
           [dwmT.bump(16)])
        cds, cdd, csl, csg = [], [], [], []
        for od in range(16):
            w = [T.at(t_att_end)]
            if od >= 2:
                w.append(T.at(cds[8 * (od - 2) + 7]))
            em("g", w, lambda g, od=od: g.dma_start(
                out=wd[:, od % 2, :, :],
                in_=dense_wT[:, 256 * od:256 * (od + 1)].rearrange(
                    "(fc p) o -> p fc o", p=128)), [dwd[od].bump(16)])
            d_wd = dwd[od].n
            for t in range(8):
                it = 8 * od + t
                w = [dwd[od].at(d_wd), V.at(cve[n_groups - 1]), S.at(cex[GJ - 1])]
                if it >= 2:
                    w.append(V.at(csg[it - 2]))
                for h in range(32):
                    em("t", w if h == 0 else [],
                       lambda tt, od=od, t=t, h=h, it=it: tt.matmul(
                           PS[:, it % 2, 0:256], ctxT[:, h, t, :], wd[:, od % 2, h, :],
                           start=(h == 0), stop=(h == 31)), [T.bump()] if h == 31 else ())
                cds.append(T.n)
                wdd = [dbt.at(16), dwmT.at(16), S.at(cex[GJ - 1])]
                if it >= 2:
                    wdd.append(S.at(csl[it - 2]))
                em("t", wdd, lambda tt, od=od, t=t, it=it: tt.matmul(
                    PS[:, 2 + it % 2, 0:256], wmT_sb[0:64, 128 * t:128 * (t + 1)],
                    bT_sb[0:64, 256 * od:256 * (od + 1)],
                    start=True, stop=True), [T.bump()])
                cdd.append(T.n)
                wsl = [T.at(cdd[it])]
                if it >= 1:
                    wsl.append(V.at(csg[it - 1]))
                em("s", wsl, lambda s, it=it: s.activation(
                    out=gsc[:], in_=PS[:, 2 + it % 2, 0:256], func=AF.Silu),
                    [S.bump()])
                csl.append(S.n)
                wsg = [T.at(cds[it]), S.at(csl[it])]
                em("v", wsg, lambda v, od=od, t=t, it=it: v.tensor_mul(
                    out=sg[:, t, 256 * od:256 * (od + 1)], in0=PS[:, it % 2, 0:256],
                    in1=gsc[:]), [V.bump()])
                csg.append(V.n)

        cfin = []
        for t in range(8):
            w = [V.at(csg[127]), T.at(cds[127])]
            if t >= 1:
                w.append(V.at(cfin[8 * t - 1]))
            em("y", w, lambda yq, t=t: yq.dma_start(
                out=xres[:], in_=x_own[128 * t:128 * (t + 1), :]), [dxr.bump(16)])
            d_x = dxr.n
            for ch in range(8):
                n = 8 * t + ch
                wv = [dxr.at(d_x)]
                if n >= 1:
                    wv.append(V.at(cfin[n - 1]))
                if n >= 2:
                    wv.append(dy[n].at(16 * (n // 2)))
                em("v", wv, lambda v, t=t, ch=ch: v.tensor_copy(
                    out=sgf[:], in_=sg[:, t, 512 * ch:512 * (ch + 1)]), [V.bump()])
                em("v", [V.now()], lambda v, n=n, ch=ch: v.tensor_add(
                    out=outb[:, n % 2, :], in0=sgf[:],
                    in1=xres[:, 512 * ch:512 * (ch + 1)]), [V.bump()])
                cfin.append(V.n)
                em("y", [V.at(cfin[n])], lambda yq, t=t, ch=ch, n=n: yq.dma_start(
                    out=y[128 * t:128 * (t + 1), 512 * ch:512 * (ch + 1)],
                    in_=outb[:, n % 2, :]), [dy[n].bump(16)])
        em("y", [], lambda yq: yq.wait_ge(dy[0].h, 32 * 16))
        em("y", [], lambda yq: yq.wait_ge(dy[1].h, 32 * 16))

        with nc.Block() as block:
            @block.sync
            def _(sync):
                for fn in OPS["y"]:
                    fn(sync)

            @block.scalar
            def _(scalar):
                for fn in OPS["s"]:
                    fn(scalar)

            @block.vector
            def _(vector):
                for fn in OPS["v"]:
                    fn(vector)

            @block.tensor
            def _(tensor):
                for fn in OPS["t"]:
                    fn(tensor)

            @block.gpsimd
            def _(gpsimd):
                for fn in OPS["g"]:
                    fn(gpsimd)
    return nc


def kernel(L0_attn_hidden_states, L0_attn_qkv_w, L0_attn_qkv_b, L0_attn_dense_w,
           L0_ffn_router_w, L0_ffn_A, L0_ffn_B):
    x = np.ascontiguousarray(np.asarray(L0_attn_hidden_states, np.float32).reshape(8192, 4096))
    qkv_wT = np.ascontiguousarray(np.asarray(L0_attn_qkv_w, np.float32).T)
    dense_wT = np.ascontiguousarray(np.asarray(L0_attn_dense_w, np.float32).T)
    Aw = np.asarray(L0_ffn_A, np.float32)
    aw_T = np.ascontiguousarray(np.concatenate(
        [Aw.transpose(2, 0, 1).reshape(4096, 64),
         np.asarray(L0_ffn_router_w, np.float32).T], axis=1))
    bT = np.ascontiguousarray(
        np.asarray(L0_ffn_B, np.float32).transpose(0, 2, 1).reshape(64, 4096))
    qkv_b = np.ascontiguousarray(np.asarray(L0_attn_qkv_b, np.float32).reshape(1, 4608))
    ident = np.eye(128, dtype=ml_dtypes.bfloat16)
    jj = np.arange(128)
    poskv = (np.arange(32)[None, :] * 128 + jj[:, None]).astype(np.float32)
    xT = np.ascontiguousarray(x.T)

    in_maps = []
    for c in range(8):
        masks = np.zeros((128, 8, 128), dtype=ml_dtypes.bfloat16)
        for w in range(8):
            masks[:, w, :] = ((128 * w + jj[:, None]) <= (8 * jj[None, :] + c)).astype(
                ml_dtypes.bfloat16)
        posq = np.zeros((128, 8), np.float32)
        for qt in range(8):
            posq[:, qt] = 1024 * (qt % 4) + 8 * jj + c
        selq = np.zeros((128, 8, 128), dtype=ml_dtypes.bfloat16)
        for q in range(8):
            tr = 8 * jj + c - 128 * q
            ok = (tr >= 0) & (tr < 128)
            selq[tr[ok], q, jj[ok]] = 1
        in_maps.append({
            "xT": xT, "xT_own": np.ascontiguousarray(xT[:, c::8]),
            "x_own": np.ascontiguousarray(x[c::8, :]),
            "qkv_wT": qkv_wT, "dense_wT": dense_wT, "aw_T": aw_T, "bT": bT,
            "qkv_b": qkv_b, "ident": ident, "masks": np.ascontiguousarray(masks),
            "posq": posq, "poskv": poskv, "selq": selq,
        })

    nc = build()
    res = run_bass_kernel_spmd(nc, in_maps, core_ids=list(range(8)))
    out = np.empty((8192, 4096), np.float32)
    for c in range(8):
        out[c::8, :] = res.results[c]["y"]
    return out.reshape(2, 4096, 4096)


# version 09
# speedup vs baseline: 1.1105x; 1.1105x over previous
import contextlib
import math

import numpy as np
import ml_dtypes
import concourse.bass as bass
import concourse.mybir as mybir
from concourse.bass_utils import run_bass_kernel_spmd

F32 = mybir.dt.float32
BF16 = mybir.dt.bfloat16
I32 = mybir.dt.int32
AF = mybir.ActivationFunctionType
ALU = mybir.AluOpType
AX = mybir.AxisListType

KB = 1024
ISQ = 1.0 / math.sqrt(128.0)
ESH = -12.0
PI = math.pi
I2PI = 1.0 / (2.0 * PI)
LN10K = math.log(10000.0)

OPS = None


def em(q, waits, fn, incs=()):
    def run(eng, waits=tuple(waits), fn=fn, incs=tuple(incs)):
        for h, v in waits:
            if v > 0:
                eng.wait_ge(h, v)
        ins = fn(eng)
        for h, k in incs:
            ins = ins.then_inc(h, k)
    OPS[q].append(run)


class Ctr:
    def __init__(self, es, nc, name):
        self.h = es.enter_context(nc.semaphore(name))
        self.n = 0

    def bump(self, k=1):
        self.n += k
        return (self.h, k)

    def at(self, v):
        return (self.h, v)

    def now(self):
        return (self.h, self.n)


class PCtr:
    def __init__(self, es, nc, name):
        self.c = [Ctr(es, nc, name + "0"), Ctr(es, nc, name + "1")]

    def __getitem__(self, i):
        return self.c[i % 2]

    def both_at(self, v):
        return [self.c[0].at(v), self.c[1].at(v)]


def build():
    global OPS
    OPS = {"t": [], "v": [], "s": [], "y": [], "g": []}
    nc = bass.Bass("TRN2", target_bir_lowering=False)

    dram = lambda name, shape, dt, kind: nc.dram_tensor(name, shape, dt, kind=kind).ap()
    xT = dram("xT", [4096, 8192], F32, "ExternalInput")
    xT_own = dram("xT_own", [4096, 1024], F32, "ExternalInput")
    x_own = dram("x_own", [1024, 4096], F32, "ExternalInput")
    qkv_wT = dram("qkv_wT", [4096, 4608], F32, "ExternalInput")
    dense_wT = dram("dense_wT", [4096, 4096], F32, "ExternalInput")
    aw_T = dram("aw_T", [4096, 68], F32, "ExternalInput")
    bTd = dram("bT", [64, 4096], F32, "ExternalInput")
    qkv_b = dram("qkv_b", [1, 4608], F32, "ExternalInput")
    ident_d = dram("ident", [128, 128], BF16, "ExternalInput")
    selq_d = dram("selq", [128, 8, 128], BF16, "ExternalInput")
    masks_d = dram("masks", [128, 8, 128], BF16, "ExternalInput")
    posq_d = dram("posq", [128, 8], F32, "ExternalInput")
    poskv_d = dram("poskv", [128, 32], F32, "ExternalInput")
    y = dram("y", [1024, 4096], F32, "ExternalOutput")
    q_d = dram("q_d", [1024, 4096], BF16, "Internal")
    k_d = dram("k_d", [8192, 256], BF16, "Internal")
    wm2_d = dram("wm2_d", [1024, 128], BF16, "Internal")

    es = contextlib.ExitStack()
    with es:
        sb = lambda name, shape, dt, off: nc.alloc_sbuf_tensor_at(name, shape, dt, offset=off).ap()
        BASE = 16512
        A, C, B, E = BASE, BASE + 65536, BASE + 98304, BASE + 206848
        xnT_blk = sb("xnT_blk", [128, 2, 32, 512], BF16, A)
        xnT_own = sb("xnT_own", [128, 32, 1024], BF16, A)
        qT = sb("qT", [128, 32, 1024], BF16, A)
        sg = sb("sg", [128, 8, 4096], BF16, A)
        v_cache = sb("v_cache", [128, 64, 2, 128], BF16, C)
        bT_sb = sb("bT_sb", [64, 4096], BF16, C)
        xf = sb("xf", [128, 2, 4, 512], F32, B)
        sqb = sb("sqb", [128, 8, 512], BF16, B + 87 * KB)
        cskv = sb("cskv", [128, 512], F32, B + 18 * KB)
        kvt1 = sb("kvt1", [128, 512], F32, B + 20 * KB)
        mrow = sb("mrow", [1, 2, 512], F32, B + 22 * KB)
        nrrow = sb("nrrow", [1, 2, 512], F32, B + 26 * KB)
        mrow_bf = sb("mrow_bf", [1, 2, 512], BF16, B + 30 * KB)
        wkv = sb("wkv", [128, 32, 512], BF16, B + 32 * KB)
        kvtab_c = sb("kvtab_c", [128, 32, 32], BF16, B + 64 * KB)
        kvtab_s = sb("kvtab_s", [128, 32, 32], BF16, B + 66 * KB)
        kvev = sb("kvev", [128, 2, 512], BF16, B + 68 * KB)
        bias_kv = sb("bias_kv", [128, 512], F32, B + 70 * KB)
        b_sb = sb("b_sb", [1, 4608], BF16, B + 72 * KB)
        ropes_kv = sb("ropes_kv", [128, 4, 128], BF16, B + 81 * KB)
        nrrow_bf = sb("nrrow_bf", [1, 2, 512], BF16, B + 82 * KB)
        m_all = sb("m_all", [128, 64], F32, B + 84 * KB)
        nr_all = sb("nr_all", [128, 64], F32, B + 84 * KB + 256)
        m_allb = sb("m_allb", [128, 64], BF16, B + 84 * KB + 512)
        nr_allb = sb("nr_allb", [128, 64], BF16, B + 84 * KB + 640)
        selq = sb("selq", [128, 8, 128], BF16, B + 85 * KB)
        wq = sb("wq", [128, 2, 32, 512], BF16, B + 32 * KB)
        csq_sb = sb("csq_sb", [128, 2, 512], F32, B + 16 * KB)
        csA_sb = sb("csA_sb", [128, 68], F32, B + 22 * KB)
        ropes_q = sb("ropes_q", [128, 4, 128], BF16, B + 22 * KB + 512)
        awT_sb = sb("awT_sb", [128, 32, 68], BF16, B + 23 * KB + 512)
        bias_q = sb("bias_q", [128, 4096], BF16, B + 96 * KB)
        qev = sb("qev", [128, 2, 512], BF16, B + 104 * KB)
        ctxT = sb("ctxT", [128, 32, 8, 128], BF16, B)
        kT = sb("kT", [128, 2, 8192], BF16, B + 64 * KB)
        probs = sb("probs", [128, 3, 1024], BF16, B + 96 * KB)
        rbc = sb("rbc", [128, 1024], F32, B + 102 * KB)
        wd = sb("wd", [128, 2, 32, 256], BF16, B + 64 * KB)
        wmT_sb = sb("wmT_sb", [128, 1024], BF16, B + 96 * KB)
        xres = sb("xres", [128, 2, 512], F32, C + 8 * KB)
        outb = sb("outb", [128, 2, 512], F32, C + 12 * KB)
        sgf = sb("sgf", [128, 512], F32, C + 16 * KB)
        masks = sb("masks_sb", [128, 8, 128], BF16, E)
        qtab_c = sb("qtab_c", [128, 8, 32], BF16, E + 2048)
        qtab_s = sb("qtab_s", [128, 8, 32], BF16, E + 2560)
        ident = sb("ident_sb", [128, 128], BF16, E + 3072)
        ones = sb("ones_sb", [128, 128], BF16, E + 3328)
        consts = sb("consts", [128, 8], F32, E + 3584)
        invfb = sb("invfb", [128, 32], F32, E + 3616)
        tabsc = sb("tabsc", [128, 4, 32], F32, E + 3744)
        tabsc_i = sb("tabsc_i", [128, 32], I32, E + 4256)
        posq = sb("posq_sb", [128, 8], F32, E + 4384)
        poskv = sb("poskv_sb", [128, 32], F32, E + 4416)
        stat = sb("stat", [128, 8], F32, E + 4544)
        mid_f = sb("mid_f", [128, 80], F32, E + 4576)
        gates = sb("gates", [128, 8, 4], F32, E + 4896)
        wmsc = sb("wmsc", [128, 128], BF16, E + 5024)
        m_own = sb("m_own", [128, 8], F32, E + 5280)
        nr_own = sb("nr_own", [128, 8], F32, E + 5312)
        gsc = sb("gsc", [128, 256], BF16, E + 5408)

        PS = nc.alloc_psum_tensor("PS", [128, 8, 512], F32).ap()

        T = Ctr(es, nc, "T")
        V = Ctr(es, nc, "V")
        S = Ctr(es, nc, "S")
        G = Ctr(es, nc, "G")
        dmisc = Ctr(es, nc, "dmisc")
        dwA = Ctr(es, nc, "dwA")
        dwq = PCtr(es, nc, "dwq")
        dwd = PCtr(es, nc, "dwd")
        dbt = Ctr(es, nc, "dbt")
        dwmT = Ctr(es, nc, "dwmT")
        dxf = PCtr(es, nc, "dxf")
        dxo = PCtr(es, nc, "dxo")
        dkd = PCtr(es, nc, "dkd")
        dqd = PCtr(es, nc, "dqd")
        dqT = Ctr(es, nc, "dqT")
        dkT = Ctr(es, nc, "dkT")
        dxr = PCtr(es, nc, "dxr")
        dwm = Ctr(es, nc, "dwm")
        dy = PCtr(es, nc, "dy")

        em("g", [], lambda g: g.dma_start(out=ident[:], in_=ident_d[:]), [dmisc.bump(16)])
        em("g", [], lambda g: g.dma_start(out=masks[:], in_=masks_d[:]), [dmisc.bump(16)])
        em("g", [], lambda g: g.dma_start(out=posq[:], in_=posq_d[:]), [dmisc.bump(16)])
        em("g", [], lambda g: g.dma_start(out=poskv[:], in_=poskv_d[:]), [dmisc.bump(16)])
        em("g", [], lambda g: g.dma_start(out=selq[:], in_=selq_d[:]), [dmisc.bump(16)])
        em("g", [], lambda g: g.dma_start(out=b_sb[:], in_=qkv_b[:]), [dwA.bump(16)])
        em("g", [], lambda g: g.dma_start(
            out=wkv[:], in_=qkv_wT[:, 4096:4608].rearrange("(fc p) o -> p fc o", p=128)),
            [dwA.bump(16)])
        em("g", [], lambda g: g.iota(tabsc[:, 0, :], pattern=[[1, 32]], base=0,
                                     channel_multiplier=0,
                                     allow_small_or_imprecise_dtypes=True), [G.bump()])

        em("v", [], lambda v: v.memset(wmsc[:], 0.0))
        em("v", [], lambda v: v.memset(consts[:, 0:1], ESH))
        em("v", [], lambda v: v.memset(consts[:, 1:2], PI / 2.0))
        em("v", [], lambda v: v.memset(consts[:, 2:3], 1e-5))
        em("v", [], lambda v: v.memset(ones[:], 1.0), [V.bump()])
        v_ones = V.n
        em("v", [G.at(1)], lambda v: v.tensor_scalar(
            out=tabsc[:, 1, :], in0=tabsc[:, 0, :], scalar1=-LN10K / 32.0, scalar2=None,
            op0=ALU.mult), [V.bump()])
        em("s", [V.now()], lambda s: s.activation(out=invfb[:], in_=tabsc[:, 1, :],
                                                  func=AF.Exp), [S.bump()])
        s_invf = S.n

        def table_ops(pos_ap, dc, ds):
            em("v", [S.at(s_invf), dmisc.at(80), S.now()], lambda v, p=pos_ap:
               v.tensor_scalar(out=tabsc[:, 1, :], in0=invfb[:], scalar1=p,
                               scalar2=None, op0=ALU.mult), [V.bump()])
            for off, dst in ((None, ds), (0.25, dc)):
                if off is None:
                    em("v", [V.now(), S.now()], lambda v: v.tensor_scalar(
                        out=tabsc[:, 2, :], in0=tabsc[:, 1, :], scalar1=I2PI,
                        scalar2=None, op0=ALU.mult), [V.bump()])
                else:
                    em("v", [V.now(), S.now()], lambda v, off=off: v.tensor_scalar(
                        out=tabsc[:, 2, :], in0=tabsc[:, 1, :], scalar1=I2PI,
                        scalar2=off, op0=ALU.mult, op1=ALU.add), [V.bump()])
                em("v", [V.now()], lambda v: v.tensor_copy(out=tabsc_i[:], in_=tabsc[:, 2, :]),
                   [V.bump()])
                em("v", [V.now()], lambda v: v.tensor_copy(out=tabsc[:, 3, :], in_=tabsc_i[:]),
                   [V.bump()])
                em("v", [V.now()], lambda v: v.tensor_sub(
                    out=tabsc[:, 2, :], in0=tabsc[:, 2, :], in1=tabsc[:, 3, :]), [V.bump()])
                em("v", [V.now()], lambda v: v.tensor_scalar(
                    out=tabsc[:, 3, :], in0=tabsc[:, 2, :], scalar1=0.5, scalar2=None,
                    op0=ALU.is_ge), [V.bump()])
                em("v", [V.now()], lambda v: v.tensor_sub(
                    out=tabsc[:, 2, :], in0=tabsc[:, 2, :], in1=tabsc[:, 3, :]), [V.bump()])
                em("v", [V.now()], lambda v: v.tensor_scalar(
                    out=tabsc[:, 2, :], in0=tabsc[:, 2, :], scalar1=2.0 * PI, scalar2=None,
                    op0=ALU.mult), [V.bump()])
                em("s", [V.now()], lambda s, d=dst: s.activation(
                    out=d, in_=tabsc[:, 2, :], func=AF.Sin), [S.bump()])

        for qt in range(8):
            table_ops(posq[:, qt:qt + 1], qtab_c[:, qt, :], qtab_s[:, qt, :])
        for kt in range(32):
            table_ops(poskv[:, kt:kt + 1], kvtab_c[:, kt, :], kvtab_s[:, kt, :])
        s_tabs = S.n

        em("t", [dwA.at(32), V.at(v_ones)], lambda t: t.matmul(
            PS[:, 0, :], ones[0:1, :], b_sb[0:1, 4096:4608], start=True, stop=True),
            [T.bump()])
        t_bkv = T.n
        em("v", [T.at(t_bkv)], lambda v: v.tensor_copy(out=bias_kv[:], in_=PS[:, 0, :]),
           [V.bump()])
        v_biaskv = V.n
        for f in range(32):
            em("t", [T.at(t_bkv)] if f == 0 else [], lambda t, f=f: t.matmul(
                PS[:, 7, :], ones[:], wkv[:, f, :], start=(f == 0), stop=(f == 31)),
                [T.bump()] if f == 31 else ())
        em("v", [T.now()], lambda v: v.tensor_copy(out=cskv[:], in_=PS[:, 7, :]),
           [V.bump()])
        v_cskv = V.n

        def rope(dst3, tabc, tabs, nh, wfirst, scratch):
            ev = dst3[:, :, 0:64:2]
            od = dst3[:, :, 1:64:2]
            cb = tabc.unsqueeze(1).broadcast_to([128, nh, 32])
            sbt = tabs.unsqueeze(1).broadcast_to([128, nh, 32])
            rr = [scratch[:, k, 0:32 * nh].rearrange("p (h i) -> p h i", h=nh)
                  for k in range(4)]
            em("v", wfirst, lambda v: v.tensor_mul(out=rr[0], in0=ev, in1=cb))
            em("v", [], lambda v: v.tensor_mul(out=rr[1], in0=od, in1=sbt))
            em("v", [], lambda v: v.tensor_mul(out=rr[2], in0=od, in1=cb))
            em("v", [], lambda v: v.tensor_mul(out=rr[3], in0=ev, in1=sbt), [V.bump()])
            em("v", [V.now()], lambda v: v.tensor_sub(out=ev, in0=rr[0], in1=rr[1]))
            em("v", [], lambda v: v.tensor_add(out=od, in0=rr[2], in1=rr[3]), [V.bump()])

        ckv, ckve, ckvr = [], [], []
        ckv_last = []
        ccast = []
        cast_l = []
        csq_last = []
        crowbf = []
        cevac = {}
        crow_read = []
        tsq = {}
        tmean_stop = {}
        cown = {}

        def emit_stats_tc(tc):
            par = tc % 2
            cc0 = {}
            for l2 in range(8):
                li = 8 * tc + l2
                wl = []
                if li >= 2:
                    wl.append(V.at(cast_l[li - 2]))
                em("y" if l2 % 2 == 0 else "s", wl,
                   lambda q, tc=tc, l2=l2, li=li: q.dma_start(
                       out=xf[:, li % 2, :, :],
                       in_=xT[512 * l2:512 * (l2 + 1),
                              512 * tc:512 * (tc + 1)].rearrange(
                           "(f p) t -> p f t", p=128)), [dxf[li].bump(16)])
                d_l = dxf[li].n
                for j in range(4):
                    fc = 4 * l2 + j
                    wv = []
                    if j == 0:
                        wv.append(dxf[li].at(d_l))
                        if l2 == 0 and tc >= 2:
                            wv += [T.at(ckv_last[tc - 2]), S.at(csq_last[tc - 2])]
                    em("v", wv, lambda v, tc=tc, li=li, j=j, fc=fc: v.tensor_copy(
                        out=xnT_blk[:, tc % 2, fc, :], in_=xf[:, li % 2, j, :]),
                        [V.bump()])
                    cc0[fc] = V.n
                    wt = [V.at(cc0[fc])]
                    if fc == 0 and tc >= 1:
                        wt.append(V.at(crow_read[tc - 1]))
                    em("t", wt, lambda t, tc=tc, fc=fc: t.matmul(
                        PS[:, 3, :], ones[:], xnT_blk[:, tc % 2, fc, :],
                        start=(fc == 0), stop=(fc == 31)),
                        [T.bump()] if fc == 31 else ())
                    if fc == 31:
                        tmean_stop[tc] = T.n
                    ws = [V.at(cc0[fc])]
                    if (tc, fc - 8) in tsq:
                        ws.append(T.at(tsq[(tc, fc - 8)]))
                    elif tc >= 1 and fc < 8:
                        ws.append(T.at(tsq[(tc - 1, 24 + fc)]))
                    em("s", ws, lambda s, tc=tc, fc=fc: s.activation(
                        out=sqb[:, fc % 8, :], in_=xnT_blk[:, tc % 2, fc, :],
                        func=AF.Square), [S.bump()])
                    s_here = S.n
                    wq4 = [S.at(s_here)]
                    if fc == 0 and tc >= 1:
                        wq4.append(V.at(crow_read[tc - 1]))
                    em("t", wq4, lambda t, tc=tc, fc=fc: t.matmul(
                        PS[:, 4, :], ones[:], sqb[:, fc % 8, :],
                        start=(fc == 0), stop=(fc == 31)), [T.bump()])
                    tsq[(tc, fc)] = T.n
                cast_l.append(V.n)
            ccast.append(V.n)
            csq_last.append(S.n)
            wv = [T.at(tmean_stop[tc])]
            if tc >= 2:
                wv.append(V.at(crow_read[tc - 2]))
            em("v", wv, lambda v, par=par: v.tensor_scalar(
                out=mrow[0:1, par, :], in0=PS[0:1, 3, :], scalar1=1.0 / 4096.0,
                scalar2=None, op0=ALU.mult), [V.bump()])
            em("v", [V.now()], lambda v, par=par: v.tensor_mul(
                out=nrrow[0:1, par, :], in0=mrow[0:1, par, :], in1=mrow[0:1, par, :]),
                [V.bump()])
            em("v", [V.now(), T.at(tsq[(tc, 31)])], lambda v, par=par:
               v.scalar_tensor_tensor(
                   out=nrrow[0:1, par, :], in0=PS[0:1, 4, :], scalar=1.0 / 4096.0,
                   in1=nrrow[0:1, par, :], op0=ALU.mult, op1=ALU.subtract), [V.bump()])
            em("s", [V.now()], lambda s, par=par: s.activation(
                out=nrrow[0:1, par, :], in_=nrrow[0:1, par, :], func=AF.Ln,
                bias=consts[0:1, 2:3]), [S.bump()])
            em("s", [S.now()], lambda s, par=par: s.activation(
                out=nrrow[0:1, par, :], in_=nrrow[0:1, par, :], func=AF.Exp,
                scale=-0.5), [S.bump()])
            em("v", [S.now()], lambda v, par=par: v.tensor_scalar(
                out=nrrow[0:1, par, :], in0=nrrow[0:1, par, :], scalar1=-1.0,
                scalar2=None, op0=ALU.mult), [V.bump()])
            em("v", [V.now()], lambda v, par=par: v.tensor_copy(
                out=mrow_bf[0:1, par, :], in_=mrow[0:1, par, :]), [V.bump()])
            em("v", [V.now()], lambda v, par=par: v.tensor_copy(
                out=nrrow_bf[0:1, par, :], in_=nrrow[0:1, par, :]), [V.bump()])
            crowbf.append(V.n)
            for kl in range(4):
                kt = 4 * tc + kl
                wtm = [V.at(crowbf[tc])]
                if kt >= 1:
                    wtm.append(V.at(cevac[kt - 1]))
                if kl == 0 and tc >= 2 and tc % 2 == 0:
                    wtm.append(V.at(cown[((tc - 2) // 2, 6)]))
                em("t", wtm, lambda t, par=par, kl=kl: t.matmul(
                    PS[:, 5, 0:1], mrow_bf[0:1, par, 128 * kl:128 * (kl + 1)],
                    ones[0:1, 0:1], start=True, stop=True), [T.bump()])
                em("t", [], lambda t, par=par, kl=kl: t.matmul(
                    PS[:, 6, 0:1], nrrow_bf[0:1, par, 128 * kl:128 * (kl + 1)],
                    ones[0:1, 0:1], start=True, stop=True), [T.bump()])
                tcol = T.n
                em("v", [T.at(tcol)], lambda v, kt=kt: v.tensor_copy(
                    out=m_all[:, kt:kt + 1], in_=PS[:, 5, 0:1]))
                em("v", [], lambda v, kt=kt: v.tensor_copy(
                    out=m_allb[:, kt:kt + 1], in_=PS[:, 5, 0:1]))
                em("v", [], lambda v, kt=kt: v.tensor_copy(
                    out=nr_all[:, kt:kt + 1], in_=PS[:, 6, 0:1]))
                em("v", [], lambda v, kt=kt: v.tensor_copy(
                    out=nr_allb[:, kt:kt + 1], in_=PS[:, 6, 0:1]), [V.bump()])
                cevac[kt] = V.n
            if tc % 2 == 1:
                t = (tc - 1) // 2
                for bank, src, dst in ((5, m_allb, m_own), (6, nr_allb, nr_own)):
                    for q in range(8):
                        kt = 8 * t + q
                        w = [V.at(cevac[4 * tc + 3]), dmisc.at(80)] if q == 0 else []
                        em("t", w, lambda tt, bank=bank, q=q, kt=kt, src=src: tt.matmul(
                            PS[:, bank, 0:1], selq[:, q, :], src[:, kt:kt + 1],
                            start=(q == 0), stop=(q == 7)),
                            [T.bump()] if q == 7 else ())
                    em("v", [T.now()], lambda v, bank=bank, t=t, dst=dst:
                       v.tensor_copy(out=dst[:, t:t + 1], in_=PS[:, bank, 0:1]),
                       [V.bump()])
                    cown[(t, bank)] = V.n
            crow_read.append(V.n)
            ckv_last.append(0)

        def emit_kv_tc(tc):
            par = tc % 2
            for kl in range(4):
                kt = 4 * tc + kl
                w = [V.at(ccast[tc]), dwA.at(32)]
                if kt < 3:
                    w.append(V.at(v_cskv))
                if kt >= 3:
                    w.append(V.at(ckve[kt - 3]))
                for f in range(32):
                    em("t", w if f == 0 else [],
                       lambda t, par=par, kl=kl, kt=kt, f=f: t.matmul(
                           PS[:, kt % 3, :],
                           xnT_blk[:, par, f, 128 * kl:128 * (kl + 1)],
                           wkv[:, f, :], start=(f == 0), stop=(f == 31)),
                       [T.bump()] if f == 31 else ())
                ckv.append(T.n)
                wv = [T.at(ckv[kt]), V.at(cevac[kt])]
                if kt >= 1:
                    wv.append(V.at(ckve[kt - 1]))
                if kt >= 2:
                    wv.append(dkd[kt].at(16 * (kt // 2)))
                em("v", wv, lambda v, kt=kt: v.scalar_tensor_tensor(
                    out=kvt1[:], in0=cskv[:], scalar=m_all[:, kt:kt + 1],
                    in1=PS[:, kt % 3, :], op0=ALU.mult, op1=ALU.subtract), [V.bump()])
                em("v", [V.now()], lambda v, kt=kt: v.scalar_tensor_tensor(
                    out=kvev[:, kt % 2, :], in0=kvt1[:], scalar=nr_all[:, kt:kt + 1],
                    in1=bias_kv[:], op0=ALU.mult, op1=ALU.add), [V.bump()])
                ckve.append(V.n)
                em("v", [S.at(s_tabs), V.at(ckve[kt])], lambda v, kt=kt: v.tensor_copy(
                    out=v_cache[:, kt, :, :],
                    in_=kvev[:, kt % 2, 256:512].rearrange("p (g d) -> p g d", g=2)))
                kview = kvev[:, kt % 2, 0:256].rearrange("p (h d) -> p h d", h=2)
                rope(kview, kvtab_c[:, kt % 32, :], kvtab_s[:, kt % 32, :], 2, [],
                     ropes_kv)
                ckvr.append(V.n)
                em("y", [V.at(ckvr[kt])], lambda yq, kt=kt: yq.dma_start(
                    out=k_d[128 * kt:128 * (kt + 1), :], in_=kvev[:, kt % 2, 0:256]),
                    [dkd[kt].bump(16)])
            ckv_last[tc] = T.n

        for tc in range(16):
            if tc >= 1:
                emit_kv_tc(tc - 1)
            emit_stats_tc(tc)
        emit_kv_tc(15)

        cast_o = []
        for st in range(16):
            l2, h = st // 2, st % 2
            wl = []
            if st == 0:
                wl += [T.at(ckv_last[15]), S.at(csq_last[15])]
            if st >= 2:
                wl.append(V.at(cast_o[st - 2]))
            if st < 2:
                wl.append(V.at(cast_l[126 + st]))
            em("y" if st % 2 == 0 else "s", wl,
               lambda q, l2=l2, h=h, st=st: q.dma_start(
                   out=xf[:, st % 2, :, :],
                   in_=xT_own[512 * l2:512 * (l2 + 1),
                              512 * h:512 * (h + 1)].rearrange(
                       "(f p) t -> p f t", p=128)), [dxo[st].bump(16)])
            d_l = dxo[st].n
            for j in range(4):
                fc = 4 * l2 + j
                em("v", [dxo[st].at(d_l)] if j == 0 else [],
                   lambda v, st=st, j=j, fc=fc, h=h: v.tensor_copy(
                       out=xnT_own[:, fc, 512 * h:512 * (h + 1)],
                       in_=xf[:, st % 2, j, :]), [V.bump()])
            cast_o.append(V.n)
        v_casto = V.n

        for ob in range(8):
            w = [V.at(ckve[63])] if ob == 0 else [V.at(V.n)]
            em("t", w, lambda t, ob=ob: t.matmul(PS[:, 5, :], ones[0:1, :],
                                                 b_sb[0:1, 512 * ob:512 * (ob + 1)],
                                                 start=True, stop=True), [T.bump()])
            em("v", [T.now()], lambda v, ob=ob: v.tensor_copy(
                out=bias_q[:, 512 * ob:512 * (ob + 1)], in_=PS[:, 5, :]), [V.bump()])
        v_qb = V.n
        t_qb2 = T.n

        em("g", [V.at(ckvr[63]), T.at(ckv_last[15])] + dkd.both_at(32 * 16),
           lambda g: g.dma_start(
            out=awT_sb[:], in_=aw_T.rearrange("(fc p) o -> p fc o", p=128)), [dwA.bump(16)])
        dwA_awt = dwA.n

        cqm, cqe, cqr = [], [], []
        ccsq = []
        for ob in range(8):
            wg = [T.at(ckv_last[15])]
            if ob < 2:
                wg += [T.at(t_qb2), V.at(ckvr[63])] + dkd.both_at(32 * 16)
            if ob >= 2:
                wg.append(T.at(cqm[8 * (ob - 2) + 7]))
            em("g", wg, lambda g, ob=ob: g.dma_start(
                out=wq[:, ob % 2, :, :],
                in_=qkv_wT[:, 512 * ob:512 * (ob + 1)].rearrange("(fc p) o -> p fc o", p=128)),
                [dwq[ob].bump(16)])
            d_wq = dwq[ob].n
            for f in range(32):
                w = [dwq[ob].at(d_wq)] if f == 0 else []
                if f == 0 and ob >= 1:
                    w.append(V.at(ccsq[ob - 1]))
                if f == 0 and ob == 0:
                    w.append(V.at(v_qb))
                em("t", w, lambda t, ob=ob, f=f: t.matmul(
                    PS[:, 4, :], ones[:], wq[:, ob % 2, f, :],
                    start=(f == 0), stop=(f == 31)), [T.bump()] if f == 31 else ())
            em("v", [T.now()] + ([V.at(ckve[63])] if ob == 0 else []),
               lambda v, ob=ob: v.tensor_copy(
                out=csq_sb[:, ob % 2, :], in_=PS[:, 4, :]), [V.bump()])
            ccsq.append(V.n)
            for t in range(8):
                n = 8 * ob + t
                w = [dwq[ob].at(d_wq), V.at(v_casto), V.at(ccsq[ob])]
                if t == 0:
                    w += [V.at(cown[(0, 5)]), V.at(cown[(7, 6)])]
                if n >= 4:
                    w.append(V.at(cqe[n - 4]))
                for f in range(32):
                    em("t", w if f == 0 else [],
                       lambda tt, ob=ob, t=t, f=f: tt.matmul(
                           PS[:, t % 4, :], xnT_own[:, f, 128 * t:128 * (t + 1)],
                           wq[:, ob % 2, f, :], start=(f == 0), stop=(f == 31)),
                       [T.bump()] if f == 31 else ())
                cqm.append(T.n)
                wv = [T.at(cqm[n])]
                if n == 0:
                    wv.append(V.at(ckve[63]))
                if n >= 1:
                    wv.append(V.at(cqe[n - 1]))
                if n >= 2:
                    wv.append(dqd[n].at(16 * (n // 2)))
                em("v", wv, lambda v, n=n, t=t, ob=ob: v.scalar_tensor_tensor(
                    out=kvt1[:], in0=csq_sb[:, ob % 2, :], scalar=m_own[:, t:t + 1],
                    in1=PS[:, t % 4, :], op0=ALU.mult, op1=ALU.subtract), [V.bump()])
                em("v", [V.now()], lambda v, n=n, t=t, ob=ob: v.scalar_tensor_tensor(
                    out=qev[:, n % 2, :], in0=kvt1[:], scalar=nr_own[:, t:t + 1],
                    in1=bias_q[:, 512 * ob:512 * (ob + 1)], op0=ALU.mult, op1=ALU.add),
                    [V.bump()])
                cqe.append(V.n)
                qview = qev[:, n % 2, :].rearrange("p (h d) -> p h d", h=4)
                rope(qview, qtab_c[:, t, :], qtab_s[:, t, :], 4, [V.at(cqe[n])],
                     ropes_q)
                cqr.append(V.n)
                em("y", [V.at(cqr[n])], lambda yq, n=n, t=t, ob=ob: yq.dma_start(
                    out=q_d[128 * t:128 * (t + 1), 512 * ob:512 * (ob + 1)],
                    in_=qev[:, n % 2, :]), [dqd[n].bump(16)])

        for f in range(32):
            w = [dwA.at(dwA_awt), V.at(ccsq[7])] if f == 0 else []
            em("t", w, lambda t, f=f: t.matmul(
                PS[:, 4, 0:68], ones[:], awT_sb[:, f, :],
                start=(f == 0), stop=(f == 31)), [T.bump()] if f == 31 else ())
        em("v", [T.now(), V.at(ckve[63])], lambda v: v.tensor_copy(
            out=csA_sb[:], in_=PS[:, 4, 0:68]), [V.bump()])
        v_csA = V.n
        cme = []
        t_mid_last = 0
        for t in range(8):
            w = [dwA.at(dwA_awt), V.at(cqe[63])]
            if t >= 2:
                w.append(V.at(cme[t - 2]))
            for f in range(32):
                em("t", w if f == 0 else [],
                   lambda tt, t=t, f=f: tt.matmul(
                       PS[:, t % 2, 0:68], xnT_own[:, f, 128 * t:128 * (t + 1)],
                       awT_sb[:, f, :], start=(f == 0), stop=(f == 31)),
                   [T.bump()] if f == 31 else ())
            t_mid_last = T.n
            wmf = [T.at(t_mid_last), V.at(v_csA)]
            if t >= 1:
                wmf.append(V.at(cme[t - 1]))
            em("v", wmf, lambda v, t=t: v.scalar_tensor_tensor(
                out=mid_f[:, 0:68], in0=csA_sb[:], scalar=m_own[:, t:t + 1],
                in1=PS[:, t % 2, 0:68], op0=ALU.mult, op1=ALU.subtract), [V.bump()])
            em("v", [V.now()], lambda v, t=t: v.tensor_scalar(
                out=mid_f[:, 0:68], in0=mid_f[:, 0:68], scalar1=nr_own[:, t:t + 1],
                scalar2=None, op0=ALU.mult), [V.bump()])
            L = lambda j: mid_f[:, 64 + j:65 + j]
            em("v", [V.now()], lambda v: v.tensor_max(out=stat[:, 0:1], in0=L(0), in1=L(1)))
            em("v", [], lambda v: v.tensor_max(out=stat[:, 1:2], in0=L(2), in1=L(3)))
            em("v", [], lambda v: v.tensor_tensor(out=stat[:, 2:3], in0=L(0), in1=L(1),
                                                  op=ALU.min))
            em("v", [], lambda v: v.tensor_tensor(out=stat[:, 3:4], in0=L(2), in1=L(3),
                                                  op=ALU.min), [V.bump()])
            em("v", [V.now()], lambda v: v.tensor_tensor(out=stat[:, 4:5], in0=stat[:, 0:1],
                                                         in1=stat[:, 1:2], op=ALU.min))
            em("v", [], lambda v: v.tensor_max(out=stat[:, 5:6], in0=stat[:, 2:3],
                                               in1=stat[:, 3:4]))
            em("v", [], lambda v: v.tensor_max(out=stat[:, 6:7], in0=stat[:, 0:1],
                                               in1=stat[:, 1:2]), [V.bump()])
            em("v", [V.now()], lambda v: v.tensor_max(out=stat[:, 4:5], in0=stat[:, 4:5],
                                                      in1=stat[:, 5:6]))
            em("v", [], lambda v: v.tensor_scalar(out=stat[:, 7:8], in0=stat[:, 6:7],
                                                  scalar1=-1.0, scalar2=None,
                                                  op0=ALU.mult), [V.bump()])
            em("s", [V.now()], lambda s: s.activation(
                out=mid_f[:, 72:76], in_=mid_f[:, 64:68], func=AF.Exp, bias=stat[:, 7:8]),
                [S.bump()])
            em("v", [S.now()], lambda v: v.tensor_scalar(
                out=mid_f[:, 76:80], in0=mid_f[:, 64:68], scalar1=stat[:, 4:5],
                scalar2=None, op0=ALU.is_ge), [V.bump()])
            em("v", [V.now()], lambda v: v.tensor_mul(
                out=mid_f[:, 72:76], in0=mid_f[:, 72:76], in1=mid_f[:, 76:80]), [V.bump()])
            em("v", [V.now()], lambda v: v.reduce_sum(
                out=stat[:, 0:1], in_=mid_f[:, 72:76], axis=AX.X), [V.bump()])
            em("v", [V.now()], lambda v: v.reciprocal(out=stat[:, 1:2], in_=stat[:, 0:1]),
               [V.bump()])
            em("v", [V.now()], lambda v, t=t: v.tensor_scalar(
                out=gates[:, t, :], in0=mid_f[:, 72:76], scalar1=stat[:, 1:2],
                scalar2=4.0, op0=ALU.mult, op1=ALU.mult), [V.bump()])
            for e in range(4):
                w = [V.now()]
                if e == 0 and t >= 1:
                    w.append(dwm.at(16 * t))
                em("v", w if e == 0 else [], lambda v, t=t, e=e: v.tensor_scalar(
                    out=wmsc[:, 16 * e:16 * (e + 1)], in0=mid_f[:, 16 * e:16 * (e + 1)],
                    scalar1=gates[:, t, e:e + 1], scalar2=None, op0=ALU.mult), [V.bump()])
            cme.append(V.n)
            v_wm = V.n
            em("g", [V.at(v_wm)], lambda g, t=t: g.dma_start(
                out=wm2_d[128 * t:128 * (t + 1), :], in_=wmsc[:]), [dwm.bump(16)])

        for g2 in range(2):
            for ss in range(8):
                w = (dkd.both_at(32 * 16) + [V.at(cqr[63]), T.at(t_mid_last)]
                     ) if (g2 == 0 and ss == 0) else []
                em("s", w, lambda q, g2=g2, ss=ss: q.dma_start(
                    out=kT[:, g2, 1024 * ss:1024 * (ss + 1)],
                    in_=k_d[1024 * ss:1024 * (ss + 1), 128 * g2:128 * (g2 + 1)],
                    transpose=True), [dkT.bump(16)])
        for h in range(32):
            w = dqd.both_at(32 * 16) + [T.at(t_mid_last)] if h == 0 else []
            em("s", w, lambda q, h=h: q.dma_start(
                out=qT[:, h, :], in_=q_d[:, 128 * h:128 * (h + 1)], transpose=True),
                [dqT.bump(16)])

        meta = []
        gidx = 0
        for qt in range(8):
            wwin, bb = qt % 4, qt // 4
            for g2 in range(2):
                for h2 in range(2):
                    hh0 = 16 * g2 + 8 * h2
                    nch = 8 * (wwin + 1)
                    for j in range(nch):
                        meta.append(dict(qt=qt, wwin=wwin, g2=g2, hh0=hh0, j=j,
                                         nch=nch, kt=32 * bb + j, gidx=gidx))
                    gidx += 1
        n_groups = gidx
        NGJ = len(meta)
        cex, csc, cat, cready = [], [], [], []
        cse, cve = [], []

        def emit_scores(GJ):
            m = meta[GJ]
            bs = 2 * (GJ % 2)
            w = []
            if GJ == 0:
                w += [dqT.at(32 * 16), dkT.at(16 * 16), V.at(cme[7]), S.at(s_tabs)]
            if GJ >= 2:
                w.append(S.at(cex[GJ - 2]))
            for half in range(2):
                em("t", w if half == 0 else [],
                   lambda t, g2=m["g2"], kt=m["kt"], qt=m["qt"], hh0=m["hh0"],
                   half=half, bs=bs:
                   t.matmul(PS[:, bs + half, :],
                            kT[:, g2, 128 * kt:128 * (kt + 1)],
                            qT[:, hh0 + 4 * half:hh0 + 4 * (half + 1),
                               128 * qt:128 * (qt + 1)],
                            start=True, stop=True),
                   [T.bump()] if half == 1 else ())
            csc.append(T.n)
            we = [T.at(csc[GJ])]
            if GJ == 0:
                we.append(V.at(cqe[63]))
            if GJ >= 3:
                we.append(T.at(cat[GJ - 3]))
            em("s", we, lambda s, GJ=GJ, bs=bs: s.activation(
                out=probs[:, GJ % 3, :],
                in_=PS[:, bs:bs + 2, :].rearrange("p b n -> p (b n)"),
                func=AF.Exp, scale=ISQ, bias=consts[:, 0:1]), [S.bump()])
            cex.append(S.n)
            if m["j"] >= 8 * m["wwin"]:
                wrel = m["j"] - 8 * m["wwin"]
                em("v", [S.at(cex[GJ])], lambda v, GJ=GJ, wrel=wrel:
                   v.tensor_mul(
                       out=probs[:, GJ % 3, :].rearrange("p (h q) -> p h q", h=8),
                       in0=probs[:, GJ % 3, :].rearrange("p (h q) -> p h q", h=8),
                       in1=masks[:, wrel, :].unsqueeze(1).broadcast_to(
                           [128, 8, 128])), [V.bump()])
                cready.append((V.h, V.n))
            else:
                cready.append((S.h, cex[GJ]))

        def emit_sumctx(GJ):
            m = meta[GJ]
            wsum = [cready[GJ]]
            if m["j"] == 0 and m["gidx"] >= 1:
                wsum += [S.at(cse[m["gidx"] - 1]), V.at(cve[m["gidx"] - 1])]
            first, last = (m["j"] == 0), (m["j"] == m["nch"] - 1)
            for half in range(2):
                em("t", wsum if half == 0 else [],
                   lambda t, GJ=GJ, half=half, first=first, last=last:
                   t.matmul(PS[:, 4 + half, :], ones[:],
                            probs[:, GJ % 3, 512 * half:512 * (half + 1)],
                            start=first, stop=last))
            for half in range(2):
                em("t", [], lambda t, GJ=GJ, half=half, kt=m["kt"], g2=m["g2"],
                   first=first, last=last:
                   t.matmul(PS[:, 6 + half, :], v_cache[:, kt, g2, :],
                            probs[:, GJ % 3, 512 * half:512 * (half + 1)],
                            start=first, stop=last),
                   [T.bump()] if half == 1 else ())
            cat.append(T.n)
            if last:
                g = m["gidx"]
                we = [T.at(cat[GJ])]
                if g == 0:
                    we += dqd.both_at(32 * 16)
                em("s", we, lambda s: s.activation(
                    out=rbc[:], in_=PS[:, 4:6, :].rearrange("p b n -> p (b n)"),
                    func=AF.Ln), [S.bump()])
                em("s", [S.now()], lambda s: s.activation(
                    out=rbc[:], in_=rbc[:], func=AF.Exp, scale=-1.0), [S.bump()])
                cse.append(S.n)
                em("v", [S.at(cse[g]), T.at(cat[GJ])],
                   lambda v, hh0=m["hh0"], qt=m["qt"]: v.tensor_mul(
                       out=ctxT[:, hh0:hh0 + 8, qt, :],
                       in0=PS[:, 6:8, :].rearrange("p b n -> p (b n)").rearrange(
                           "p (h q) -> p h q", h=8),
                       in1=rbc[:].rearrange("p (h q) -> p h q", h=8)), [V.bump()])
                cve.append(V.n)

        for GJ in range(NGJ + 1):
            if GJ < NGJ:
                emit_scores(GJ)
            if GJ >= 1:
                emit_sumctx(GJ - 1)
        t_att_end = T.n
        s_att_end = S.n

        em("g", [T.at(t_att_end)], lambda g: g.dma_start(
            out=bT_sb[:], in_=bTd[:]), [dbt.bump(16)])
        em("s", [dwm.at(8 * 16), T.at(t_att_end), V.at(cve[n_groups - 1])],
           lambda q: q.dma_start(out=wmT_sb[:], in_=wm2_d[:], transpose=True),
           [dwmT.bump(16)])
        cds, cdd, csl, csg, cfin = [], [], [], [], []
        for od in range(16):
            w = [T.at(t_att_end)]
            if od >= 2:
                w.append(T.at(cds[8 * (od - 2) + 7]))
            em("g", w, lambda g, od=od: g.dma_start(
                out=wd[:, od % 2, :, :],
                in_=dense_wT[:, 256 * od:256 * (od + 1)].rearrange(
                    "(fc p) o -> p fc o", p=128)), [dwd[od].bump(16)])
            d_wd = dwd[od].n
            for t in range(8):
                it = 8 * od + t
                w = [dwd[od].at(d_wd), V.at(cve[n_groups - 1]), S.at(cex[GJ - 1])]
                if it >= 2:
                    w.append(V.at(csg[it - 2]))
                for h in range(32):
                    em("t", w if h == 0 else [],
                       lambda tt, od=od, t=t, h=h, it=it: tt.matmul(
                           PS[:, it % 2, 0:256], ctxT[:, h, t, :], wd[:, od % 2, h, :],
                           start=(h == 0), stop=(h == 31)), [T.bump()] if h == 31 else ())
                cds.append(T.n)
                wdd = [dbt.at(16), dwmT.at(16), S.at(cex[GJ - 1])]
                if it >= 2:
                    wdd.append(S.at(csl[it - 2]))
                em("t", wdd, lambda tt, od=od, t=t, it=it: tt.matmul(
                    PS[:, 2 + it % 2, 0:256], wmT_sb[0:64, 128 * t:128 * (t + 1)],
                    bT_sb[0:64, 256 * od:256 * (od + 1)],
                    start=True, stop=True), [T.bump()])
                cdd.append(T.n)
                wsl = [T.at(cdd[it])]
                if it >= 1:
                    wsl.append(V.at(csg[it - 1]))
                em("s", wsl, lambda s, it=it: s.activation(
                    out=gsc[:], in_=PS[:, 2 + it % 2, 0:256], func=AF.Silu),
                    [S.bump()])
                csl.append(S.n)
                wsg = [T.at(cds[it]), S.at(csl[it])]
                em("v", wsg, lambda v, od=od, t=t, it=it: v.tensor_mul(
                    out=sg[:, t, 256 * od:256 * (od + 1)], in0=PS[:, it % 2, 0:256],
                    in1=gsc[:]), [V.bump()])
                csg.append(V.n)
            if od % 2 == 1:
                ch = od // 2
                for t in range(8):
                    fin = 8 * ch + t
                    wload = [V.at(csg[8 * (2 * ch) + t])]
                    if fin >= 2:
                        wload.append(V.at(cfin[fin - 2]))
                    em("y", wload, lambda yq, t=t, ch=ch, fin=fin: yq.dma_start(
                        out=xres[:, fin % 2, :],
                        in_=x_own[128 * t:128 * (t + 1), 512 * ch:512 * (ch + 1)]),
                        [dxr[fin].bump(16)])
                    d_x = dxr[fin].n
                    wv = [dxr[fin].at(d_x), V.at(csg[8 * (2 * ch + 1) + t])]
                    if fin >= 1:
                        wv.append(V.at(cfin[fin - 1]))
                    if fin >= 2:
                        wv.append(dy[fin].at(16 * (fin // 2)))
                    em("v", wv, lambda v, t=t, ch=ch: v.tensor_copy(
                        out=sgf[:], in_=sg[:, t, 512 * ch:512 * (ch + 1)]), [V.bump()])
                    em("v", [V.now()], lambda v, fin=fin: v.tensor_add(
                        out=outb[:, fin % 2, :], in0=sgf[:],
                        in1=xres[:, fin % 2, :]), [V.bump()])
                    cfin.append(V.n)
                    em("y", [V.at(cfin[fin])], lambda yq, t=t, ch=ch, fin=fin:
                       yq.dma_start(
                           out=y[128 * t:128 * (t + 1), 512 * ch:512 * (ch + 1)],
                           in_=outb[:, fin % 2, :]), [dy[fin].bump(16)])
        em("y", [], lambda yq: yq.wait_ge(dy[0].h, 32 * 16))
        em("y", [], lambda yq: yq.wait_ge(dy[1].h, 32 * 16))

        with nc.Block() as block:
            @block.sync
            def _(sync):
                for fn in OPS["y"]:
                    fn(sync)

            @block.scalar
            def _(scalar):
                for fn in OPS["s"]:
                    fn(scalar)

            @block.vector
            def _(vector):
                for fn in OPS["v"]:
                    fn(vector)

            @block.tensor
            def _(tensor):
                for fn in OPS["t"]:
                    fn(tensor)

            @block.gpsimd
            def _(gpsimd):
                for fn in OPS["g"]:
                    fn(gpsimd)
    return nc


def kernel(L0_attn_hidden_states, L0_attn_qkv_w, L0_attn_qkv_b, L0_attn_dense_w,
           L0_ffn_router_w, L0_ffn_A, L0_ffn_B):
    x = np.ascontiguousarray(np.asarray(L0_attn_hidden_states, np.float32).reshape(8192, 4096))
    qkv_wT = np.ascontiguousarray(np.asarray(L0_attn_qkv_w, np.float32).T)
    dense_wT = np.ascontiguousarray(np.asarray(L0_attn_dense_w, np.float32).T)
    Aw = np.asarray(L0_ffn_A, np.float32)
    aw_T = np.ascontiguousarray(np.concatenate(
        [Aw.transpose(2, 0, 1).reshape(4096, 64),
         np.asarray(L0_ffn_router_w, np.float32).T], axis=1))
    bT = np.ascontiguousarray(
        np.asarray(L0_ffn_B, np.float32).transpose(0, 2, 1).reshape(64, 4096))
    qkv_b = np.ascontiguousarray(np.asarray(L0_attn_qkv_b, np.float32).reshape(1, 4608))
    ident = np.eye(128, dtype=ml_dtypes.bfloat16)
    jj = np.arange(128)
    poskv = (np.arange(32)[None, :] * 128 + jj[:, None]).astype(np.float32)
    xT = np.ascontiguousarray(x.T)

    in_maps = []
    for c in range(8):
        masks = np.zeros((128, 8, 128), dtype=ml_dtypes.bfloat16)
        for w in range(8):
            masks[:, w, :] = ((128 * w + jj[:, None]) <= (8 * jj[None, :] + c)).astype(
                ml_dtypes.bfloat16)
        posq = np.zeros((128, 8), np.float32)
        for qt in range(8):
            posq[:, qt] = 1024 * (qt % 4) + 8 * jj + c
        selq = np.zeros((128, 8, 128), dtype=ml_dtypes.bfloat16)
        for q in range(8):
            tr = 8 * jj + c - 128 * q
            ok = (tr >= 0) & (tr < 128)
            selq[tr[ok], q, jj[ok]] = 1
        in_maps.append({
            "xT": xT, "xT_own": np.ascontiguousarray(xT[:, c::8]),
            "x_own": np.ascontiguousarray(x[c::8, :]),
            "qkv_wT": qkv_wT, "dense_wT": dense_wT, "aw_T": aw_T, "bT": bT,
            "qkv_b": qkv_b, "ident": ident, "masks": np.ascontiguousarray(masks),
            "posq": posq, "poskv": poskv, "selq": selq,
        })

    nc = build()
    res = run_bass_kernel_spmd(nc, in_maps, core_ids=list(range(8)))
    out = np.empty((8192, 4096), np.float32)
    for c in range(8):
        out[c::8, :] = res.results[c]["y"]
    return out.reshape(2, 4096, 4096)


# version 11
# speedup vs baseline: 1.1617x; 1.0461x over previous
import contextlib
import math

import numpy as np
import ml_dtypes
import concourse.bass as bass
import concourse.mybir as mybir
from concourse.bass_utils import run_bass_kernel_spmd

F32 = mybir.dt.float32
BF16 = mybir.dt.bfloat16
I32 = mybir.dt.int32
AF = mybir.ActivationFunctionType
ALU = mybir.AluOpType
AX = mybir.AxisListType

KB = 1024
ISQ = 1.0 / math.sqrt(128.0)
ESH = -12.0
PI = math.pi
I2PI = 1.0 / (2.0 * PI)
LN10K = math.log(10000.0)

OPS = None


def em(q, waits, fn, incs=()):
    def run(eng, waits=tuple(waits), fn=fn, incs=tuple(incs)):
        for h, v in waits:
            if v > 0:
                eng.wait_ge(h, v)
        ins = fn(eng)
        for h, k in incs:
            ins = ins.then_inc(h, k)
    OPS[q].append(run)


class Ctr:
    def __init__(self, es, nc, name):
        self.h = es.enter_context(nc.semaphore(name))
        self.n = 0

    def bump(self, k=1):
        self.n += k
        return (self.h, k)

    def at(self, v):
        return (self.h, v)

    def now(self):
        return (self.h, self.n)


class PCtr:
    def __init__(self, es, nc, name):
        self.c = [Ctr(es, nc, name + "0"), Ctr(es, nc, name + "1")]

    def __getitem__(self, i):
        return self.c[i % 2]

    def both_at(self, v):
        return [self.c[0].at(v), self.c[1].at(v)]


def build():
    global OPS
    OPS = {"t": [], "v": [], "s": [], "y": [], "g": []}
    nc = bass.Bass("TRN2", target_bir_lowering=False)

    dram = lambda name, shape, dt, kind: nc.dram_tensor(name, shape, dt, kind=kind).ap()
    xT = dram("xT", [4096, 8192], F32, "ExternalInput")
    xT_own = dram("xT_own", [4096, 1024], F32, "ExternalInput")
    x_own = dram("x_own", [1024, 4096], F32, "ExternalInput")
    qkv_wT = dram("qkv_wT", [4096, 4608], F32, "ExternalInput")
    dense_wT = dram("dense_wT", [4096, 4096], F32, "ExternalInput")
    aw_T = dram("aw_T", [4096, 68], F32, "ExternalInput")
    bTd = dram("bT", [64, 4096], F32, "ExternalInput")
    qkv_b = dram("qkv_b", [1, 4608], F32, "ExternalInput")
    ident_d = dram("ident", [128, 128], BF16, "ExternalInput")
    selq_d = dram("selq", [128, 8, 128], BF16, "ExternalInput")
    masks_d = dram("masks", [128, 8, 128], BF16, "ExternalInput")
    posq_d = dram("posq", [128, 8], F32, "ExternalInput")
    poskv_d = dram("poskv", [128, 32], F32, "ExternalInput")
    y = dram("y", [1024, 4096], F32, "ExternalOutput")
    q_d = dram("q_d", [1024, 4096], BF16, "Internal")
    k_d = dram("k_d", [8192, 256], BF16, "Internal")
    wm2_d = dram("wm2_d", [1024, 128], BF16, "Internal")

    es = contextlib.ExitStack()
    with es:
        sb = lambda name, shape, dt, off: nc.alloc_sbuf_tensor_at(name, shape, dt, offset=off).ap()
        BASE = 16512
        A, C, B, E = BASE, BASE + 65536, BASE + 98304, BASE + 206848
        xnT_blk = sb("xnT_blk", [128, 2, 32, 512], BF16, A)
        xnT_own = sb("xnT_own", [128, 32, 1024], BF16, A)
        qT = sb("qT", [128, 32, 1024], BF16, A)
        sg = sb("sg", [128, 8, 4096], BF16, A)
        v_cache = sb("v_cache", [128, 64, 2, 128], BF16, C)
        bT_sb = sb("bT_sb", [64, 4096], BF16, C)
        xf = sb("xf", [128, 2, 4, 512], F32, B)
        sqb = sb("sqb", [128, 8, 512], BF16, B + 87 * KB)
        cskv = sb("cskv", [128, 512], F32, B + 18 * KB)
        kvt1 = sb("kvt1", [128, 512], F32, B + 20 * KB)
        mrow = sb("mrow", [1, 2, 512], F32, B + 22 * KB)
        nrrow = sb("nrrow", [1, 2, 512], F32, B + 26 * KB)
        mrow_bf = sb("mrow_bf", [1, 2, 512], BF16, B + 30 * KB)
        wkv = sb("wkv", [128, 32, 512], BF16, B + 32 * KB)
        kvtab_c = sb("kvtab_c", [128, 32, 32], BF16, B + 64 * KB)
        kvtab_s = sb("kvtab_s", [128, 32, 32], BF16, B + 66 * KB)
        kvev = sb("kvev", [128, 2, 512], BF16, B + 68 * KB)
        bias_kv = sb("bias_kv", [128, 512], F32, B + 70 * KB)
        b_sb = sb("b_sb", [1, 4608], BF16, B + 72 * KB)
        ropes_kv = sb("ropes_kv", [128, 4, 128], BF16, B + 81 * KB)
        nrrow_bf = sb("nrrow_bf", [1, 2, 512], BF16, B + 82 * KB)
        m_all = sb("m_all", [128, 64], F32, B + 84 * KB)
        nr_all = sb("nr_all", [128, 64], F32, B + 84 * KB + 256)
        m_allb = sb("m_allb", [128, 64], BF16, B + 84 * KB + 512)
        nr_allb = sb("nr_allb", [128, 64], BF16, B + 84 * KB + 640)
        selq = sb("selq", [128, 8, 128], BF16, B + 85 * KB)
        wq = sb("wq", [128, 2, 32, 512], BF16, B + 32 * KB)
        csq_sb = sb("csq_sb", [128, 2, 512], F32, B + 16 * KB)
        csA_sb = sb("csA_sb", [128, 68], F32, B + 22 * KB)
        ropes_q = sb("ropes_q", [128, 4, 128], BF16, B + 22 * KB + 512)
        awT_sb = sb("awT_sb", [128, 32, 68], BF16, B + 23 * KB + 512)
        bias_q = sb("bias_q", [128, 4096], BF16, B + 96 * KB)
        qev = sb("qev", [128, 2, 512], BF16, B + 104 * KB)
        ctxT = sb("ctxT", [128, 32, 8, 128], BF16, B)
        kT = sb("kT", [128, 2, 8192], BF16, B + 64 * KB)
        probs = sb("probs", [128, 3, 1024], BF16, B + 96 * KB)
        rbc = sb("rbc", [128, 1024], F32, B + 102 * KB)
        wd = sb("wd", [128, 2, 32, 256], BF16, B + 64 * KB)
        wmT_sb = sb("wmT_sb", [128, 1024], BF16, B + 96 * KB)
        xres = sb("xres", [128, 4096], F32, B)
        outb = sb("outb", [128, 2, 512], F32, B + 64 * KB)
        sgf = sb("sgf", [128, 512], F32, B + 68 * KB)
        masks = sb("masks_sb", [128, 8, 128], BF16, E)
        qtab_c = sb("qtab_c", [128, 8, 32], BF16, E + 2048)
        qtab_s = sb("qtab_s", [128, 8, 32], BF16, E + 2560)
        ident = sb("ident_sb", [128, 128], BF16, E + 3072)
        ones = sb("ones_sb", [128, 128], BF16, E + 3328)
        consts = sb("consts", [128, 8], F32, E + 3584)
        invfb = sb("invfb", [128, 32], F32, E + 3616)
        tabsc = sb("tabsc", [128, 4, 32], F32, E + 3744)
        tabsc_i = sb("tabsc_i", [128, 32], I32, E + 4256)
        posq = sb("posq_sb", [128, 8], F32, E + 4384)
        poskv = sb("poskv_sb", [128, 32], F32, E + 4416)
        stat = sb("stat", [128, 8], F32, E + 4544)
        mid_f = sb("mid_f", [128, 80], F32, E + 4576)
        gates = sb("gates", [128, 8, 4], F32, E + 4896)
        wmsc = sb("wmsc", [128, 128], BF16, E + 5024)
        m_own = sb("m_own", [128, 8], F32, E + 5280)
        nr_own = sb("nr_own", [128, 8], F32, E + 5312)
        gsc = sb("gsc", [128, 256], BF16, E + 5408)

        PS = nc.alloc_psum_tensor("PS", [128, 8, 512], F32).ap()

        T = Ctr(es, nc, "T")
        V = Ctr(es, nc, "V")
        S = Ctr(es, nc, "S")
        G = Ctr(es, nc, "G")
        dmisc = Ctr(es, nc, "dmisc")
        dwA = Ctr(es, nc, "dwA")
        dwq = PCtr(es, nc, "dwq")
        dwd = PCtr(es, nc, "dwd")
        dbt = Ctr(es, nc, "dbt")
        dwmT = Ctr(es, nc, "dwmT")
        dxf = PCtr(es, nc, "dxf")
        dxo = PCtr(es, nc, "dxo")
        dkd = PCtr(es, nc, "dkd")
        dqd = PCtr(es, nc, "dqd")
        dqT = Ctr(es, nc, "dqT")
        dkT = Ctr(es, nc, "dkT")
        dxr = Ctr(es, nc, "dxr")
        dwm = Ctr(es, nc, "dwm")
        dy = PCtr(es, nc, "dy")

        em("g", [], lambda g: g.dma_start(out=ident[:], in_=ident_d[:]), [dmisc.bump(16)])
        em("g", [], lambda g: g.dma_start(out=masks[:], in_=masks_d[:]), [dmisc.bump(16)])
        em("g", [], lambda g: g.dma_start(out=posq[:], in_=posq_d[:]), [dmisc.bump(16)])
        em("g", [], lambda g: g.dma_start(out=poskv[:], in_=poskv_d[:]), [dmisc.bump(16)])
        em("g", [], lambda g: g.dma_start(out=selq[:], in_=selq_d[:]), [dmisc.bump(16)])
        em("g", [], lambda g: g.dma_start(out=b_sb[:], in_=qkv_b[:]), [dwA.bump(16)])
        em("g", [], lambda g: g.dma_start(
            out=wkv[:], in_=qkv_wT[:, 4096:4608].rearrange("(fc p) o -> p fc o", p=128)),
            [dwA.bump(16)])
        em("g", [], lambda g: g.iota(tabsc[:, 0, :], pattern=[[1, 32]], base=0,
                                     channel_multiplier=0,
                                     allow_small_or_imprecise_dtypes=True), [G.bump()])

        em("v", [], lambda v: v.memset(wmsc[:], 0.0))
        em("v", [], lambda v: v.memset(consts[:, 0:1], ESH))
        em("v", [], lambda v: v.memset(consts[:, 1:2], PI / 2.0))
        em("v", [], lambda v: v.memset(consts[:, 2:3], 1e-5))
        em("v", [], lambda v: v.memset(ones[:], 1.0), [V.bump()])
        v_ones = V.n
        em("v", [G.at(1)], lambda v: v.tensor_scalar(
            out=tabsc[:, 1, :], in0=tabsc[:, 0, :], scalar1=-LN10K / 32.0, scalar2=None,
            op0=ALU.mult), [V.bump()])
        em("s", [V.now()], lambda s: s.activation(out=invfb[:], in_=tabsc[:, 1, :],
                                                  func=AF.Exp), [S.bump()])
        s_invf = S.n

        def table_ops(pos_ap, dc, ds):
            em("v", [S.at(s_invf), dmisc.at(80), S.now()], lambda v, p=pos_ap:
               v.tensor_scalar(out=tabsc[:, 1, :], in0=invfb[:], scalar1=p,
                               scalar2=None, op0=ALU.mult), [V.bump()])
            for off, dst in ((None, ds), (0.25, dc)):
                if off is None:
                    em("v", [V.now(), S.now()], lambda v: v.tensor_scalar(
                        out=tabsc[:, 2, :], in0=tabsc[:, 1, :], scalar1=I2PI,
                        scalar2=None, op0=ALU.mult), [V.bump()])
                else:
                    em("v", [V.now(), S.now()], lambda v, off=off: v.tensor_scalar(
                        out=tabsc[:, 2, :], in0=tabsc[:, 1, :], scalar1=I2PI,
                        scalar2=off, op0=ALU.mult, op1=ALU.add), [V.bump()])
                em("v", [V.now()], lambda v: v.tensor_copy(out=tabsc_i[:], in_=tabsc[:, 2, :]),
                   [V.bump()])
                em("v", [V.now()], lambda v: v.tensor_copy(out=tabsc[:, 3, :], in_=tabsc_i[:]),
                   [V.bump()])
                em("v", [V.now()], lambda v: v.tensor_sub(
                    out=tabsc[:, 2, :], in0=tabsc[:, 2, :], in1=tabsc[:, 3, :]), [V.bump()])
                em("v", [V.now()], lambda v: v.tensor_scalar(
                    out=tabsc[:, 3, :], in0=tabsc[:, 2, :], scalar1=0.5, scalar2=None,
                    op0=ALU.is_ge), [V.bump()])
                em("v", [V.now()], lambda v: v.tensor_sub(
                    out=tabsc[:, 2, :], in0=tabsc[:, 2, :], in1=tabsc[:, 3, :]), [V.bump()])
                em("v", [V.now()], lambda v: v.tensor_scalar(
                    out=tabsc[:, 2, :], in0=tabsc[:, 2, :], scalar1=2.0 * PI, scalar2=None,
                    op0=ALU.mult), [V.bump()])
                em("s", [V.now()], lambda s, d=dst: s.activation(
                    out=d, in_=tabsc[:, 2, :], func=AF.Sin), [S.bump()])

        for qt in range(8):
            table_ops(posq[:, qt:qt + 1], qtab_c[:, qt, :], qtab_s[:, qt, :])
        for kt in range(32):
            table_ops(poskv[:, kt:kt + 1], kvtab_c[:, kt, :], kvtab_s[:, kt, :])
        s_tabs = S.n

        em("t", [dwA.at(32), V.at(v_ones)], lambda t: t.matmul(
            PS[:, 0, :], ones[0:1, :], b_sb[0:1, 4096:4608], start=True, stop=True),
            [T.bump()])
        t_bkv = T.n
        em("v", [T.at(t_bkv)], lambda v: v.tensor_copy(out=bias_kv[:], in_=PS[:, 0, :]),
           [V.bump()])
        v_biaskv = V.n
        for f in range(32):
            em("t", [T.at(t_bkv)] if f == 0 else [], lambda t, f=f: t.matmul(
                PS[:, 7, :], ones[:], wkv[:, f, :], start=(f == 0), stop=(f == 31)),
                [T.bump()] if f == 31 else ())
        em("v", [T.now()], lambda v: v.tensor_copy(out=cskv[:], in_=PS[:, 7, :]),
           [V.bump()])
        v_cskv = V.n

        def rope(dst3, tabc, tabs, nh, wfirst, scratch):
            ev = dst3[:, :, 0:64:2]
            od = dst3[:, :, 1:64:2]
            cb = tabc.unsqueeze(1).broadcast_to([128, nh, 32])
            sbt = tabs.unsqueeze(1).broadcast_to([128, nh, 32])
            rr = [scratch[:, k, 0:32 * nh].rearrange("p (h i) -> p h i", h=nh)
                  for k in range(4)]
            em("v", wfirst, lambda v: v.tensor_mul(out=rr[0], in0=ev, in1=cb))
            em("v", [], lambda v: v.tensor_mul(out=rr[1], in0=od, in1=sbt))
            em("v", [], lambda v: v.tensor_mul(out=rr[2], in0=od, in1=cb))
            em("v", [], lambda v: v.tensor_mul(out=rr[3], in0=ev, in1=sbt), [V.bump()])
            em("v", [V.now()], lambda v: v.tensor_sub(out=ev, in0=rr[0], in1=rr[1]))
            em("v", [], lambda v: v.tensor_add(out=od, in0=rr[2], in1=rr[3]), [V.bump()])

        ckv, ckve, ckvr = [], [], []
        ckv_last = []
        ccast = []
        cast_l = []
        csq_last = []
        crowbf = []
        cevac = {}
        crow_read = []
        tsq = {}
        tmean_stop = {}
        cown = {}

        def emit_stats_tc(tc):
            par = tc % 2
            for l2 in range(4):
                wl = []
                if l2 == 0 and tc >= 2:
                    wl += [T.at(ckv_last[tc - 2]), S.at(csq_last[tc - 2])]
                em("g", wl, lambda q, tc=tc, l2=l2: q.dma_start(
                    out=xnT_blk[:, tc % 2, 8 * l2:8 * (l2 + 1), :],
                    in_=xT[1024 * l2:1024 * (l2 + 1),
                           512 * tc:512 * (tc + 1)].rearrange(
                        "(f p) t -> p f t", p=128)), [dxf[tc].bump(16)])
            d_tc = dxf[tc].n
            ccast.append(d_tc)
            for fc in range(32):
                wt = []
                if fc == 0:
                    wt.append(dxf[tc].at(d_tc))
                    if tc >= 1:
                        wt.append(V.at(crow_read[tc - 1]))
                em("t", wt, lambda t, tc=tc, fc=fc: t.matmul(
                    PS[:, 3, :], ones[:], xnT_blk[:, tc % 2, fc, :],
                    start=(fc == 0), stop=(fc == 31)),
                    [T.bump()] if fc == 31 else ())
                if fc == 31:
                    tmean_stop[tc] = T.n
                ws = []
                if fc == 0:
                    ws.append(dxf[tc].at(d_tc))
                if (tc, fc - 8) in tsq:
                    ws.append(T.at(tsq[(tc, fc - 8)]))
                elif tc >= 1 and fc < 8:
                    ws.append(T.at(tsq[(tc - 1, 24 + fc)]))
                em("s", ws, lambda s, tc=tc, fc=fc: s.activation(
                    out=sqb[:, fc % 8, :], in_=xnT_blk[:, tc % 2, fc, :],
                    func=AF.Square), [S.bump()])
                s_here = S.n
                wq4 = [S.at(s_here)]
                if fc == 0 and tc >= 1:
                    wq4.append(V.at(crow_read[tc - 1]))
                em("t", wq4, lambda t, tc=tc, fc=fc: t.matmul(
                    PS[:, 4, :], ones[:], sqb[:, fc % 8, :],
                    start=(fc == 0), stop=(fc == 31)), [T.bump()])
                tsq[(tc, fc)] = T.n
            csq_last.append(S.n)
            wv = [T.at(tmean_stop[tc])]
            if tc >= 2:
                wv.append(V.at(crow_read[tc - 2]))
            em("v", wv, lambda v, par=par: v.tensor_scalar(
                out=mrow[0:1, par, :], in0=PS[0:1, 3, :], scalar1=1.0 / 4096.0,
                scalar2=None, op0=ALU.mult), [V.bump()])
            em("v", [V.now()], lambda v, par=par: v.tensor_mul(
                out=nrrow[0:1, par, :], in0=mrow[0:1, par, :], in1=mrow[0:1, par, :]),
                [V.bump()])
            em("v", [V.now(), T.at(tsq[(tc, 31)])], lambda v, par=par:
               v.scalar_tensor_tensor(
                   out=nrrow[0:1, par, :], in0=PS[0:1, 4, :], scalar=1.0 / 4096.0,
                   in1=nrrow[0:1, par, :], op0=ALU.mult, op1=ALU.subtract), [V.bump()])
            em("s", [V.now()], lambda s, par=par: s.activation(
                out=nrrow[0:1, par, :], in_=nrrow[0:1, par, :], func=AF.Ln,
                bias=consts[0:1, 2:3]), [S.bump()])
            em("s", [S.now()], lambda s, par=par: s.activation(
                out=nrrow[0:1, par, :], in_=nrrow[0:1, par, :], func=AF.Exp,
                scale=-0.5), [S.bump()])
            em("v", [S.now()], lambda v, par=par: v.tensor_scalar(
                out=nrrow[0:1, par, :], in0=nrrow[0:1, par, :], scalar1=-1.0,
                scalar2=None, op0=ALU.mult), [V.bump()])
            em("v", [V.now()], lambda v, par=par: v.tensor_copy(
                out=mrow_bf[0:1, par, :], in_=mrow[0:1, par, :]), [V.bump()])
            em("v", [V.now()], lambda v, par=par: v.tensor_copy(
                out=nrrow_bf[0:1, par, :], in_=nrrow[0:1, par, :]), [V.bump()])
            crowbf.append(V.n)
            for kl in range(4):
                kt = 4 * tc + kl
                wtm = [V.at(crowbf[tc])]
                if kt >= 1:
                    wtm.append(V.at(cevac[kt - 1]))
                if kl == 0 and tc >= 2 and tc % 2 == 0:
                    wtm.append(V.at(cown[((tc - 2) // 2, 6)]))
                em("t", wtm, lambda t, par=par, kl=kl: t.matmul(
                    PS[:, 5, 0:1], mrow_bf[0:1, par, 128 * kl:128 * (kl + 1)],
                    ones[0:1, 0:1], start=True, stop=True), [T.bump()])
                em("t", [], lambda t, par=par, kl=kl: t.matmul(
                    PS[:, 6, 0:1], nrrow_bf[0:1, par, 128 * kl:128 * (kl + 1)],
                    ones[0:1, 0:1], start=True, stop=True), [T.bump()])
                tcol = T.n
                em("v", [T.at(tcol)], lambda v, kt=kt: v.tensor_copy(
                    out=m_all[:, kt:kt + 1], in_=PS[:, 5, 0:1]))
                em("v", [], lambda v, kt=kt: v.tensor_copy(
                    out=m_allb[:, kt:kt + 1], in_=PS[:, 5, 0:1]))
                em("v", [], lambda v, kt=kt: v.tensor_copy(
                    out=nr_all[:, kt:kt + 1], in_=PS[:, 6, 0:1]))
                em("v", [], lambda v, kt=kt: v.tensor_copy(
                    out=nr_allb[:, kt:kt + 1], in_=PS[:, 6, 0:1]), [V.bump()])
                cevac[kt] = V.n
            if tc % 2 == 1:
                t = (tc - 1) // 2
                for bank, src, dst in ((5, m_allb, m_own), (6, nr_allb, nr_own)):
                    for q in range(8):
                        kt = 8 * t + q
                        w = [V.at(cevac[4 * tc + 3]), dmisc.at(80)] if q == 0 else []
                        em("t", w, lambda tt, bank=bank, q=q, kt=kt, src=src: tt.matmul(
                            PS[:, bank, 0:1], selq[:, q, :], src[:, kt:kt + 1],
                            start=(q == 0), stop=(q == 7)),
                            [T.bump()] if q == 7 else ())
                    em("v", [T.now()], lambda v, bank=bank, t=t, dst=dst:
                       v.tensor_copy(out=dst[:, t:t + 1], in_=PS[:, bank, 0:1]),
                       [V.bump()])
                    cown[(t, bank)] = V.n
            crow_read.append(V.n)
            ckv_last.append(0)

        def emit_kv_tc(tc):
            par = tc % 2
            for kl in range(4):
                kt = 4 * tc + kl
                w = [dxf[tc].at(ccast[tc]), dwA.at(32)]
                if kt < 3:
                    w.append(V.at(v_cskv))
                if kt >= 3:
                    w.append(V.at(ckve[kt - 3]))
                for f in range(32):
                    em("t", w if f == 0 else [],
                       lambda t, par=par, kl=kl, kt=kt, f=f: t.matmul(
                           PS[:, kt % 3, :],
                           xnT_blk[:, par, f, 128 * kl:128 * (kl + 1)],
                           wkv[:, f, :], start=(f == 0), stop=(f == 31)),
                       [T.bump()] if f == 31 else ())
                ckv.append(T.n)
                wv = [T.at(ckv[kt]), V.at(cevac[kt])]
                if kt >= 1:
                    wv.append(V.at(ckve[kt - 1]))
                if kt >= 2:
                    wv.append(dkd[kt].at(16 * (kt // 2)))
                em("v", wv, lambda v, kt=kt: v.scalar_tensor_tensor(
                    out=kvt1[:], in0=cskv[:], scalar=m_all[:, kt:kt + 1],
                    in1=PS[:, kt % 3, :], op0=ALU.mult, op1=ALU.subtract), [V.bump()])
                em("v", [V.now()], lambda v, kt=kt: v.scalar_tensor_tensor(
                    out=kvev[:, kt % 2, :], in0=kvt1[:], scalar=nr_all[:, kt:kt + 1],
                    in1=bias_kv[:], op0=ALU.mult, op1=ALU.add), [V.bump()])
                ckve.append(V.n)
                em("v", [S.at(s_tabs), V.at(ckve[kt])], lambda v, kt=kt: v.tensor_copy(
                    out=v_cache[:, kt, :, :],
                    in_=kvev[:, kt % 2, 256:512].rearrange("p (g d) -> p g d", g=2)))
                kview = kvev[:, kt % 2, 0:256].rearrange("p (h d) -> p h d", h=2)
                rope(kview, kvtab_c[:, kt % 32, :], kvtab_s[:, kt % 32, :], 2, [],
                     ropes_kv)
                ckvr.append(V.n)
                em("y", [V.at(ckvr[kt])], lambda yq, kt=kt: yq.dma_start(
                    out=k_d[128 * kt:128 * (kt + 1), :], in_=kvev[:, kt % 2, 0:256]),
                    [dkd[kt].bump(16)])
            ckv_last[tc] = T.n

        for tc in range(16):
            if tc >= 1:
                emit_kv_tc(tc - 1)
            emit_stats_tc(tc)
        emit_kv_tc(15)

        for l2 in range(8):
            wl = [T.at(ckv_last[15]), S.at(csq_last[15])] if l2 == 0 else []
            em("g", wl, lambda q, l2=l2: q.dma_start(
                out=xnT_own[:, 4 * l2:4 * (l2 + 1), :],
                in_=xT_own[512 * l2:512 * (l2 + 1), :].rearrange(
                    "(f p) t -> p f t", p=128)), [dxo[0].bump(16)])
        d_own = dxo[0].n

        for ob in range(8):
            w = [V.at(ckve[63])] if ob == 0 else [V.at(V.n)]
            em("t", w, lambda t, ob=ob: t.matmul(PS[:, 5, :], ones[0:1, :],
                                                 b_sb[0:1, 512 * ob:512 * (ob + 1)],
                                                 start=True, stop=True), [T.bump()])
            em("v", [T.now()], lambda v, ob=ob: v.tensor_copy(
                out=bias_q[:, 512 * ob:512 * (ob + 1)], in_=PS[:, 5, :]), [V.bump()])
        v_qb = V.n
        t_qb2 = T.n

        em("g", [V.at(ckvr[63]), T.at(ckv_last[15])] + dkd.both_at(32 * 16),
           lambda g: g.dma_start(
            out=awT_sb[:], in_=aw_T.rearrange("(fc p) o -> p fc o", p=128)), [dwA.bump(16)])
        dwA_awt = dwA.n

        cqm, cqe, cqr = [], [], []
        ccsq = []
        for ob in range(8):
            wg = [T.at(ckv_last[15])]
            if ob < 2:
                wg += [T.at(t_qb2), V.at(ckvr[63])] + dkd.both_at(32 * 16)
            if ob >= 2:
                wg.append(T.at(cqm[8 * (ob - 2) + 7]))
            em("g", wg, lambda g, ob=ob: g.dma_start(
                out=wq[:, ob % 2, :, :],
                in_=qkv_wT[:, 512 * ob:512 * (ob + 1)].rearrange("(fc p) o -> p fc o", p=128)),
                [dwq[ob].bump(16)])
            d_wq = dwq[ob].n
            for f in range(32):
                w = [dwq[ob].at(d_wq)] if f == 0 else []
                if f == 0 and ob >= 1:
                    w.append(V.at(ccsq[ob - 1]))
                if f == 0 and ob == 0:
                    w.append(V.at(v_qb))
                em("t", w, lambda t, ob=ob, f=f: t.matmul(
                    PS[:, 4, :], ones[:], wq[:, ob % 2, f, :],
                    start=(f == 0), stop=(f == 31)), [T.bump()] if f == 31 else ())
            em("v", [T.now()] + ([V.at(ckve[63])] if ob == 0 else []),
               lambda v, ob=ob: v.tensor_copy(
                out=csq_sb[:, ob % 2, :], in_=PS[:, 4, :]), [V.bump()])
            ccsq.append(V.n)
            for t in range(8):
                n = 8 * ob + t
                w = [dwq[ob].at(d_wq), dxo[0].at(d_own), V.at(ccsq[ob])]
                if ob == 0 and t == 0:
                    w.append(V.at(ckve[63]))
                if t == 0:
                    w += [V.at(cown[(0, 5)]), V.at(cown[(7, 6)])]
                if n >= 4:
                    w.append(V.at(cqe[n - 4]))
                for f in range(32):
                    em("t", w if f == 0 else [],
                       lambda tt, ob=ob, t=t, f=f: tt.matmul(
                           PS[:, t % 4, :], xnT_own[:, f, 128 * t:128 * (t + 1)],
                           wq[:, ob % 2, f, :], start=(f == 0), stop=(f == 31)),
                       [T.bump()] if f == 31 else ())
                cqm.append(T.n)
                wv = [T.at(cqm[n])]
                if n == 0:
                    wv.append(V.at(ckve[63]))
                if n >= 1:
                    wv.append(V.at(cqe[n - 1]))
                if n >= 2:
                    wv.append(dqd[n].at(16 * (n // 2)))
                em("v", wv, lambda v, n=n, t=t, ob=ob: v.scalar_tensor_tensor(
                    out=kvt1[:], in0=csq_sb[:, ob % 2, :], scalar=m_own[:, t:t + 1],
                    in1=PS[:, t % 4, :], op0=ALU.mult, op1=ALU.subtract), [V.bump()])
                em("v", [V.now()], lambda v, n=n, t=t, ob=ob: v.scalar_tensor_tensor(
                    out=qev[:, n % 2, :], in0=kvt1[:], scalar=nr_own[:, t:t + 1],
                    in1=bias_q[:, 512 * ob:512 * (ob + 1)], op0=ALU.mult, op1=ALU.add),
                    [V.bump()])
                cqe.append(V.n)
                qview = qev[:, n % 2, :].rearrange("p (h d) -> p h d", h=4)
                rope(qview, qtab_c[:, t, :], qtab_s[:, t, :], 4, [V.at(cqe[n])],
                     ropes_q)
                cqr.append(V.n)
                em("y", [V.at(cqr[n])], lambda yq, n=n, t=t, ob=ob: yq.dma_start(
                    out=q_d[128 * t:128 * (t + 1), 512 * ob:512 * (ob + 1)],
                    in_=qev[:, n % 2, :]), [dqd[n].bump(16)])

        for f in range(32):
            w = [dwA.at(dwA_awt), V.at(ccsq[7])] if f == 0 else []
            em("t", w, lambda t, f=f: t.matmul(
                PS[:, 4, 0:68], ones[:], awT_sb[:, f, :],
                start=(f == 0), stop=(f == 31)), [T.bump()] if f == 31 else ())
        em("v", [T.now(), V.at(ckve[63])], lambda v: v.tensor_copy(
            out=csA_sb[:], in_=PS[:, 4, 0:68]), [V.bump()])
        v_csA = V.n
        cme = []
        t_mid_last = 0
        for t in range(8):
            w = [dwA.at(dwA_awt), V.at(cqe[63])]
            if t >= 2:
                w.append(V.at(cme[t - 2]))
            for f in range(32):
                em("t", w if f == 0 else [],
                   lambda tt, t=t, f=f: tt.matmul(
                       PS[:, t % 2, 0:68], xnT_own[:, f, 128 * t:128 * (t + 1)],
                       awT_sb[:, f, :], start=(f == 0), stop=(f == 31)),
                   [T.bump()] if f == 31 else ())
            t_mid_last = T.n
            wmf = [T.at(t_mid_last), V.at(v_csA)]
            if t >= 1:
                wmf.append(V.at(cme[t - 1]))
            em("v", wmf, lambda v, t=t: v.scalar_tensor_tensor(
                out=mid_f[:, 0:68], in0=csA_sb[:], scalar=m_own[:, t:t + 1],
                in1=PS[:, t % 2, 0:68], op0=ALU.mult, op1=ALU.subtract), [V.bump()])
            em("v", [V.now()], lambda v, t=t: v.tensor_scalar(
                out=mid_f[:, 0:68], in0=mid_f[:, 0:68], scalar1=nr_own[:, t:t + 1],
                scalar2=None, op0=ALU.mult), [V.bump()])
            L = lambda j: mid_f[:, 64 + j:65 + j]
            em("v", [V.now()], lambda v: v.tensor_max(out=stat[:, 0:1], in0=L(0), in1=L(1)))
            em("v", [], lambda v: v.tensor_max(out=stat[:, 1:2], in0=L(2), in1=L(3)))
            em("v", [], lambda v: v.tensor_tensor(out=stat[:, 2:3], in0=L(0), in1=L(1),
                                                  op=ALU.min))
            em("v", [], lambda v: v.tensor_tensor(out=stat[:, 3:4], in0=L(2), in1=L(3),
                                                  op=ALU.min), [V.bump()])
            em("v", [V.now()], lambda v: v.tensor_tensor(out=stat[:, 4:5], in0=stat[:, 0:1],
                                                         in1=stat[:, 1:2], op=ALU.min))
            em("v", [], lambda v: v.tensor_max(out=stat[:, 5:6], in0=stat[:, 2:3],
                                               in1=stat[:, 3:4]))
            em("v", [], lambda v: v.tensor_max(out=stat[:, 6:7], in0=stat[:, 0:1],
                                               in1=stat[:, 1:2]), [V.bump()])
            em("v", [V.now()], lambda v: v.tensor_max(out=stat[:, 4:5], in0=stat[:, 4:5],
                                                      in1=stat[:, 5:6]))
            em("v", [], lambda v: v.tensor_scalar(out=stat[:, 7:8], in0=stat[:, 6:7],
                                                  scalar1=-1.0, scalar2=None,
                                                  op0=ALU.mult), [V.bump()])
            em("s", [V.now()], lambda s: s.activation(
                out=mid_f[:, 72:76], in_=mid_f[:, 64:68], func=AF.Exp, bias=stat[:, 7:8]),
                [S.bump()])
            em("v", [S.now()], lambda v: v.tensor_scalar(
                out=mid_f[:, 76:80], in0=mid_f[:, 64:68], scalar1=stat[:, 4:5],
                scalar2=None, op0=ALU.is_ge), [V.bump()])
            em("v", [V.now()], lambda v: v.tensor_mul(
                out=mid_f[:, 72:76], in0=mid_f[:, 72:76], in1=mid_f[:, 76:80]), [V.bump()])
            em("v", [V.now()], lambda v: v.reduce_sum(
                out=stat[:, 0:1], in_=mid_f[:, 72:76], axis=AX.X), [V.bump()])
            em("v", [V.now()], lambda v: v.reciprocal(out=stat[:, 1:2], in_=stat[:, 0:1]),
               [V.bump()])
            em("v", [V.now()], lambda v, t=t: v.tensor_scalar(
                out=gates[:, t, :], in0=mid_f[:, 72:76], scalar1=stat[:, 1:2],
                scalar2=4.0, op0=ALU.mult, op1=ALU.mult), [V.bump()])
            for e in range(4):
                w = [V.now()]
                if e == 0 and t >= 1:
                    w.append(dwm.at(16 * t))
                em("v", w if e == 0 else [], lambda v, t=t, e=e: v.tensor_scalar(
                    out=wmsc[:, 16 * e:16 * (e + 1)], in0=mid_f[:, 16 * e:16 * (e + 1)],
                    scalar1=gates[:, t, e:e + 1], scalar2=None, op0=ALU.mult), [V.bump()])
            cme.append(V.n)
            v_wm = V.n
            em("g", [V.at(v_wm)], lambda g, t=t: g.dma_start(
                out=wm2_d[128 * t:128 * (t + 1), :], in_=wmsc[:]), [dwm.bump(16)])

        for g2 in range(2):
            for ss in range(8):
                w = (dkd.both_at(32 * 16) + [V.at(cqr[63]), T.at(t_mid_last)]
                     ) if (g2 == 0 and ss == 0) else []
                em("s", w, lambda q, g2=g2, ss=ss: q.dma_start(
                    out=kT[:, g2, 1024 * ss:1024 * (ss + 1)],
                    in_=k_d[1024 * ss:1024 * (ss + 1), 128 * g2:128 * (g2 + 1)],
                    transpose=True), [dkT.bump(16)])
        for h in range(32):
            w = dqd.both_at(32 * 16) + [T.at(t_mid_last)] if h == 0 else []
            em("s", w, lambda q, h=h: q.dma_start(
                out=qT[:, h, :], in_=q_d[:, 128 * h:128 * (h + 1)], transpose=True),
                [dqT.bump(16)])

        meta = []
        gidx = 0
        for qt in range(8):
            wwin, bb = qt % 4, qt // 4
            for g2 in range(2):
                for h2 in range(2):
                    hh0 = 16 * g2 + 8 * h2
                    nch = 8 * (wwin + 1)
                    for j in range(nch):
                        meta.append(dict(qt=qt, wwin=wwin, g2=g2, hh0=hh0, j=j,
                                         nch=nch, kt=32 * bb + j, gidx=gidx))
                    gidx += 1
        n_groups = gidx
        NGJ = len(meta)
        cex, csc, cat, cready = [], [], [], []
        cse, cve = [], []

        def emit_scores(GJ):
            m = meta[GJ]
            bs = 2 * (GJ % 2)
            w = []
            if GJ == 0:
                w += [dqT.at(32 * 16), dkT.at(16 * 16), V.at(cme[7]), S.at(s_tabs)]
            if GJ >= 2:
                w.append(S.at(cex[GJ - 2]))
            for half in range(2):
                em("t", w if half == 0 else [],
                   lambda t, g2=m["g2"], kt=m["kt"], qt=m["qt"], hh0=m["hh0"],
                   half=half, bs=bs:
                   t.matmul(PS[:, bs + half, :],
                            kT[:, g2, 128 * kt:128 * (kt + 1)],
                            qT[:, hh0 + 4 * half:hh0 + 4 * (half + 1),
                               128 * qt:128 * (qt + 1)],
                            start=True, stop=True),
                   [T.bump()] if half == 1 else ())
            csc.append(T.n)
            we = [T.at(csc[GJ])]
            if GJ == 0:
                we.append(V.at(cqe[63]))
            if GJ >= 3:
                we.append(T.at(cat[GJ - 3]))
            em("s", we, lambda s, GJ=GJ, bs=bs: s.activation(
                out=probs[:, GJ % 3, :],
                in_=PS[:, bs:bs + 2, :].rearrange("p b n -> p (b n)"),
                func=AF.Exp, scale=ISQ, bias=consts[:, 0:1]), [S.bump()])
            cex.append(S.n)
            if m["j"] >= 8 * m["wwin"]:
                wrel = m["j"] - 8 * m["wwin"]
                em("v", [S.at(cex[GJ])], lambda v, GJ=GJ, wrel=wrel:
                   v.tensor_mul(
                       out=probs[:, GJ % 3, :].rearrange("p (h q) -> p h q", h=8),
                       in0=probs[:, GJ % 3, :].rearrange("p (h q) -> p h q", h=8),
                       in1=masks[:, wrel, :].unsqueeze(1).broadcast_to(
                           [128, 8, 128])), [V.bump()])
                cready.append((V.h, V.n))
            else:
                cready.append((S.h, cex[GJ]))

        def emit_sumctx(GJ):
            m = meta[GJ]
            wsum = [cready[GJ]]
            if m["j"] == 0 and m["gidx"] >= 1:
                wsum += [S.at(cse[m["gidx"] - 1]), V.at(cve[m["gidx"] - 1])]
            first, last = (m["j"] == 0), (m["j"] == m["nch"] - 1)
            for half in range(2):
                em("t", wsum if half == 0 else [],
                   lambda t, GJ=GJ, half=half, first=first, last=last:
                   t.matmul(PS[:, 4 + half, :], ones[:],
                            probs[:, GJ % 3, 512 * half:512 * (half + 1)],
                            start=first, stop=last))
            for half in range(2):
                em("t", [], lambda t, GJ=GJ, half=half, kt=m["kt"], g2=m["g2"],
                   first=first, last=last:
                   t.matmul(PS[:, 6 + half, :], v_cache[:, kt, g2, :],
                            probs[:, GJ % 3, 512 * half:512 * (half + 1)],
                            start=first, stop=last),
                   [T.bump()] if half == 1 else ())
            cat.append(T.n)
            if last:
                g = m["gidx"]
                we = [T.at(cat[GJ])]
                if g == 0:
                    we += dqd.both_at(32 * 16)
                em("s", we, lambda s: s.activation(
                    out=rbc[:], in_=PS[:, 4:6, :].rearrange("p b n -> p (b n)"),
                    func=AF.Ln), [S.bump()])
                em("s", [S.now()], lambda s: s.activation(
                    out=rbc[:], in_=rbc[:], func=AF.Exp, scale=-1.0), [S.bump()])
                cse.append(S.n)
                em("v", [S.at(cse[g]), T.at(cat[GJ])],
                   lambda v, hh0=m["hh0"], qt=m["qt"]: v.tensor_mul(
                       out=ctxT[:, hh0:hh0 + 8, qt, :],
                       in0=PS[:, 6:8, :].rearrange("p b n -> p (b n)").rearrange(
                           "p (h q) -> p h q", h=8),
                       in1=rbc[:].rearrange("p (h q) -> p h q", h=8)), [V.bump()])
                cve.append(V.n)

        for GJ in range(NGJ + 1):
            if GJ < NGJ:
                emit_scores(GJ)
            if GJ >= 1:
                emit_sumctx(GJ - 1)
        t_att_end = T.n
        s_att_end = S.n

        em("g", [T.at(t_att_end)], lambda g: g.dma_start(
            out=bT_sb[:], in_=bTd[:]), [dbt.bump(16)])
        em("s", [dwm.at(8 * 16), T.at(t_att_end), V.at(cve[n_groups - 1])],
           lambda q: q.dma_start(out=wmT_sb[:], in_=wm2_d[:], transpose=True),
           [dwmT.bump(16)])
        cds, cdd, csl, csg = [], [], [], []
        for od in range(16):
            w = [T.at(t_att_end)]
            if od >= 2:
                w.append(T.at(cds[8 * (od - 2) + 7]))
            em("g", w, lambda g, od=od: g.dma_start(
                out=wd[:, od % 2, :, :],
                in_=dense_wT[:, 256 * od:256 * (od + 1)].rearrange(
                    "(fc p) o -> p fc o", p=128)), [dwd[od].bump(16)])
            d_wd = dwd[od].n
            for t in range(8):
                it = 8 * od + t
                w = [dwd[od].at(d_wd), V.at(cve[n_groups - 1]), S.at(cex[GJ - 1])]
                if it >= 2:
                    w.append(V.at(csg[it - 2]))
                for h in range(32):
                    em("t", w if h == 0 else [],
                       lambda tt, od=od, t=t, h=h, it=it: tt.matmul(
                           PS[:, it % 2, 0:256], ctxT[:, h, t, :], wd[:, od % 2, h, :],
                           start=(h == 0), stop=(h == 31)), [T.bump()] if h == 31 else ())
                cds.append(T.n)
                wdd = [dbt.at(16), dwmT.at(16), S.at(cex[GJ - 1])]
                if it >= 2:
                    wdd.append(S.at(csl[it - 2]))
                em("t", wdd, lambda tt, od=od, t=t, it=it: tt.matmul(
                    PS[:, 2 + it % 2, 0:256], wmT_sb[0:64, 128 * t:128 * (t + 1)],
                    bT_sb[0:64, 256 * od:256 * (od + 1)],
                    start=True, stop=True), [T.bump()])
                cdd.append(T.n)
                wsl = [T.at(cdd[it])]
                if it >= 1:
                    wsl.append(V.at(csg[it - 1]))
                em("s", wsl, lambda s, it=it: s.activation(
                    out=gsc[:], in_=PS[:, 2 + it % 2, 0:256], func=AF.Silu),
                    [S.bump()])
                csl.append(S.n)
                wsg = [T.at(cds[it]), S.at(csl[it])]
                em("v", wsg, lambda v, od=od, t=t, it=it: v.tensor_mul(
                    out=sg[:, t, 256 * od:256 * (od + 1)], in0=PS[:, it % 2, 0:256],
                    in1=gsc[:]), [V.bump()])
                csg.append(V.n)

        cfin = []
        for t in range(8):
            w = [V.at(csg[127]), T.at(cds[127])]
            if t >= 1:
                w.append(V.at(cfin[8 * t - 1]))
            em("y", w, lambda yq, t=t: yq.dma_start(
                out=xres[:], in_=x_own[128 * t:128 * (t + 1), :]), [dxr.bump(16)])
            d_x = dxr.n
            for ch in range(8):
                n = 8 * t + ch
                wv = [dxr.at(d_x)]
                if n >= 1:
                    wv.append(V.at(cfin[n - 1]))
                if n >= 2:
                    wv.append(dy[n].at(16 * (n // 2)))
                em("v", wv, lambda v, t=t, ch=ch: v.tensor_copy(
                    out=sgf[:], in_=sg[:, t, 512 * ch:512 * (ch + 1)]), [V.bump()])
                em("v", [V.now()], lambda v, n=n, ch=ch: v.tensor_add(
                    out=outb[:, n % 2, :], in0=sgf[:],
                    in1=xres[:, 512 * ch:512 * (ch + 1)]), [V.bump()])
                cfin.append(V.n)
                em("y", [V.at(cfin[n])], lambda yq, t=t, ch=ch, n=n: yq.dma_start(
                    out=y[128 * t:128 * (t + 1), 512 * ch:512 * (ch + 1)],
                    in_=outb[:, n % 2, :]), [dy[n].bump(16)])
        em("y", [], lambda yq: yq.wait_ge(dy[0].h, 32 * 16))
        em("y", [], lambda yq: yq.wait_ge(dy[1].h, 32 * 16))

        with nc.Block() as block:
            @block.sync
            def _(sync):
                for fn in OPS["y"]:
                    fn(sync)

            @block.scalar
            def _(scalar):
                for fn in OPS["s"]:
                    fn(scalar)

            @block.vector
            def _(vector):
                for fn in OPS["v"]:
                    fn(vector)

            @block.tensor
            def _(tensor):
                for fn in OPS["t"]:
                    fn(tensor)

            @block.gpsimd
            def _(gpsimd):
                for fn in OPS["g"]:
                    fn(gpsimd)
    return nc


def kernel(L0_attn_hidden_states, L0_attn_qkv_w, L0_attn_qkv_b, L0_attn_dense_w,
           L0_ffn_router_w, L0_ffn_A, L0_ffn_B):
    x = np.ascontiguousarray(np.asarray(L0_attn_hidden_states, np.float32).reshape(8192, 4096))
    qkv_wT = np.ascontiguousarray(np.asarray(L0_attn_qkv_w, np.float32).T)
    dense_wT = np.ascontiguousarray(np.asarray(L0_attn_dense_w, np.float32).T)
    Aw = np.asarray(L0_ffn_A, np.float32)
    aw_T = np.ascontiguousarray(np.concatenate(
        [Aw.transpose(2, 0, 1).reshape(4096, 64),
         np.asarray(L0_ffn_router_w, np.float32).T], axis=1))
    bT = np.ascontiguousarray(
        np.asarray(L0_ffn_B, np.float32).transpose(0, 2, 1).reshape(64, 4096))
    qkv_b = np.ascontiguousarray(np.asarray(L0_attn_qkv_b, np.float32).reshape(1, 4608))
    ident = np.eye(128, dtype=ml_dtypes.bfloat16)
    jj = np.arange(128)
    poskv = (np.arange(32)[None, :] * 128 + jj[:, None]).astype(np.float32)
    xT = np.ascontiguousarray(x.T)

    in_maps = []
    for c in range(8):
        masks = np.zeros((128, 8, 128), dtype=ml_dtypes.bfloat16)
        for w in range(8):
            masks[:, w, :] = ((128 * w + jj[:, None]) <= (8 * jj[None, :] + c)).astype(
                ml_dtypes.bfloat16)
        posq = np.zeros((128, 8), np.float32)
        for qt in range(8):
            posq[:, qt] = 1024 * (qt % 4) + 8 * jj + c
        selq = np.zeros((128, 8, 128), dtype=ml_dtypes.bfloat16)
        for q in range(8):
            tr = 8 * jj + c - 128 * q
            ok = (tr >= 0) & (tr < 128)
            selq[tr[ok], q, jj[ok]] = 1
        in_maps.append({
            "xT": xT, "xT_own": np.ascontiguousarray(xT[:, c::8]),
            "x_own": np.ascontiguousarray(x[c::8, :]),
            "qkv_wT": qkv_wT, "dense_wT": dense_wT, "aw_T": aw_T, "bT": bT,
            "qkv_b": qkv_b, "ident": ident, "masks": np.ascontiguousarray(masks),
            "posq": posq, "poskv": poskv, "selq": selq,
        })

    nc = build()
    res = run_bass_kernel_spmd(nc, in_maps, core_ids=list(range(8)))
    out = np.empty((8192, 4096), np.float32)
    for c in range(8):
        out[c::8, :] = res.results[c]["y"]
    return out.reshape(2, 4096, 4096)


# version 12 (final)
# speedup vs baseline: 1.4465x; 1.2452x over previous
import contextlib
import math

import numpy as np
import ml_dtypes
import concourse.bass as bass
import concourse.mybir as mybir
from concourse.bass_utils import run_bass_kernel_spmd

F32 = mybir.dt.float32
BF16 = mybir.dt.bfloat16
I32 = mybir.dt.int32
AF = mybir.ActivationFunctionType
ALU = mybir.AluOpType
AX = mybir.AxisListType

KB = 1024
ISQ = 1.0 / math.sqrt(128.0)
ESH = -12.0
PI = math.pi
I2PI = 1.0 / (2.0 * PI)
LN10K = math.log(10000.0)

OPS = None


def em(q, waits, fn, incs=()):
    def run(eng, waits=tuple(waits), fn=fn, incs=tuple(incs)):
        for h, v in waits:
            if v > 0:
                eng.wait_ge(h, v)
        ins = fn(eng)
        for h, k in incs:
            ins = ins.then_inc(h, k)
    OPS[q].append(run)


class Ctr:
    def __init__(self, es, nc, name):
        self.h = es.enter_context(nc.semaphore(name))
        self.n = 0

    def bump(self, k=1):
        self.n += k
        return (self.h, k)

    def at(self, v):
        return (self.h, v)

    def now(self):
        return (self.h, self.n)


class PCtr:
    def __init__(self, es, nc, name):
        self.c = [Ctr(es, nc, name + "0"), Ctr(es, nc, name + "1")]

    def __getitem__(self, i):
        return self.c[i % 2]

    def both_at(self, v):
        return [self.c[0].at(v), self.c[1].at(v)]


def build():
    global OPS
    OPS = {"t": [], "v": [], "s": [], "y": [], "g": []}
    nc = bass.Bass("TRN2", target_bir_lowering=False)

    dram = lambda name, shape, dt, kind: nc.dram_tensor(name, shape, dt, kind=kind).ap()
    xT = dram("xT", [4096, 8192], F32, "ExternalInput")
    xT_own = dram("xT_own", [4096, 1024], F32, "ExternalInput")
    x_own = dram("x_own", [1024, 4096], F32, "ExternalInput")
    qkv_wT = dram("qkv_wT", [4096, 4608], F32, "ExternalInput")
    dense_wT = dram("dense_wT", [4096, 4096], F32, "ExternalInput")
    aw_T = dram("aw_T", [4096, 68], F32, "ExternalInput")
    bTd = dram("bT", [64, 4096], F32, "ExternalInput")
    qkv_b = dram("qkv_b", [1, 4608], F32, "ExternalInput")
    ident_d = dram("ident", [128, 128], BF16, "ExternalInput")
    selq_d = dram("selq", [128, 8, 128], BF16, "ExternalInput")
    masks_d = dram("masks", [128, 8, 128], BF16, "ExternalInput")
    posq_d = dram("posq", [128, 8], F32, "ExternalInput")
    poskv_d = dram("poskv", [128, 32], F32, "ExternalInput")
    y = dram("y", [1024, 4096], F32, "ExternalOutput")
    q_d = dram("q_d", [1024, 4096], BF16, "Internal")
    k_d = dram("k_d", [8192, 256], BF16, "Internal")
    wm2_d = dram("wm2_d", [1024, 128], BF16, "Internal")

    es = contextlib.ExitStack()
    with es:
        sb = lambda name, shape, dt, off: nc.alloc_sbuf_tensor_at(name, shape, dt, offset=off).ap()
        BASE = 16512
        A, C, B, E = BASE, BASE + 65536, BASE + 98304, BASE + 206848
        xnT_blk = sb("xnT_blk", [128, 2, 32, 512], BF16, A)
        xnT_own = sb("xnT_own", [128, 32, 1024], BF16, A)
        qT = sb("qT", [128, 32, 1024], BF16, A)
        sg = sb("sg", [128, 8, 4096], BF16, A)
        v_cache = sb("v_cache", [128, 64, 2, 128], BF16, C)
        bT_sb = sb("bT_sb", [64, 4096], BF16, C)
        xf = sb("xf", [128, 2, 4, 512], F32, B)
        sqb = sb("sqb", [128, 8, 512], BF16, B + 87 * KB)
        cskv = sb("cskv", [128, 512], F32, B + 18 * KB)
        kvt1 = sb("kvt1", [128, 512], F32, B + 20 * KB)
        mrow = sb("mrow", [1, 2, 512], F32, B + 22 * KB)
        nrrow = sb("nrrow", [1, 2, 512], F32, B + 26 * KB)
        mrow_bf = sb("mrow_bf", [1, 2, 512], BF16, B + 30 * KB)
        wkv = sb("wkv", [128, 32, 512], BF16, B + 32 * KB)
        kvtab_c = sb("kvtab_c", [128, 32, 32], BF16, B + 64 * KB)
        kvtab_s = sb("kvtab_s", [128, 32, 32], BF16, B + 66 * KB)
        kvev = sb("kvev", [128, 2, 512], BF16, B + 68 * KB)
        bias_kv = sb("bias_kv", [128, 512], F32, B + 70 * KB)
        b_sb = sb("b_sb", [1, 4608], BF16, B + 72 * KB)
        ropes_kv = sb("ropes_kv", [128, 4, 128], BF16, B + 81 * KB)
        nrrow_bf = sb("nrrow_bf", [1, 2, 512], BF16, B + 82 * KB)
        m_all = sb("m_all", [128, 64], F32, B + 84 * KB)
        nr_all = sb("nr_all", [128, 64], F32, B + 84 * KB + 256)
        m_allb = sb("m_allb", [128, 64], BF16, B + 84 * KB + 512)
        nr_allb = sb("nr_allb", [128, 64], BF16, B + 84 * KB + 640)
        selq = sb("selq", [128, 8, 128], BF16, B + 85 * KB)
        wq = sb("wq", [128, 2, 32, 512], BF16, B + 32 * KB)
        csq_sb = sb("csq_sb", [128, 2, 512], F32, B + 16 * KB)
        csA_sb = sb("csA_sb", [128, 68], F32, B + 22 * KB)
        ropes_q = sb("ropes_q", [128, 4, 128], BF16, B + 22 * KB + 512)
        awT_sb = sb("awT_sb", [128, 32, 68], BF16, B + 23 * KB + 512)
        bias_q = sb("bias_q", [128, 4096], BF16, B + 96 * KB)
        qev = sb("qev", [128, 2, 512], BF16, B + 104 * KB)
        wtA = sb("wtA", [128, 32, 32], F32, B + 96 * KB)
        wtB = sb("wtB", [128, 32, 32], F32, B + 100 * KB)
        wtI = sb("wtI", [128, 32, 32], I32, B + 87 * KB)
        wtF = sb("wtF", [128, 32, 32], F32, B + 91 * KB)
        ctxT = sb("ctxT", [128, 32, 8, 128], BF16, B)
        kT = sb("kT", [128, 2, 8192], BF16, B + 64 * KB)
        probs = sb("probs", [128, 3, 1024], BF16, B + 96 * KB)
        rbc = sb("rbc", [128, 1024], F32, B + 102 * KB)
        wd = sb("wd", [128, 2, 32, 256], BF16, B + 64 * KB)
        wmT_sb = sb("wmT_sb", [128, 1024], BF16, B + 96 * KB)
        xres = sb("xres", [128, 4096], F32, B)
        outb = sb("outb", [128, 2, 512], F32, B + 64 * KB)
        sgf = sb("sgf", [128, 512], F32, B + 68 * KB)
        masks = sb("masks_sb", [128, 8, 128], BF16, E)
        qtab_c = sb("qtab_c", [128, 8, 32], BF16, E + 2048)
        qtab_s = sb("qtab_s", [128, 8, 32], BF16, E + 2560)
        ident = sb("ident_sb", [128, 128], BF16, E + 3072)
        ones = sb("ones_sb", [128, 128], BF16, E + 3328)
        consts = sb("consts", [128, 8], F32, E + 3584)
        invfb = sb("invfb", [128, 32], F32, E + 3616)
        tabsc = sb("tabsc", [128, 4, 32], F32, E + 3744)
        tabsc_i = sb("tabsc_i", [128, 32], I32, E + 4256)
        posq = sb("posq_sb", [128, 8], F32, E + 4384)
        poskv = sb("poskv_sb", [128, 32], F32, E + 4416)
        stat = sb("stat", [128, 8], F32, E + 4544)
        mid_f = sb("mid_f", [128, 80], F32, E + 4576)
        gates = sb("gates", [128, 8, 4], F32, E + 4896)
        wmsc = sb("wmsc", [128, 128], BF16, E + 5024)
        m_own = sb("m_own", [128, 8], F32, E + 5280)
        nr_own = sb("nr_own", [128, 8], F32, E + 5312)
        gsc = sb("gsc", [128, 256], BF16, E + 5408)

        PS = nc.alloc_psum_tensor("PS", [128, 8, 512], F32).ap()

        T = Ctr(es, nc, "T")
        V = Ctr(es, nc, "V")
        S = Ctr(es, nc, "S")
        G = Ctr(es, nc, "G")
        dmisc = Ctr(es, nc, "dmisc")
        dwA = Ctr(es, nc, "dwA")
        dwq = PCtr(es, nc, "dwq")
        dwd = PCtr(es, nc, "dwd")
        dbt = Ctr(es, nc, "dbt")
        dwmT = Ctr(es, nc, "dwmT")
        dxf = PCtr(es, nc, "dxf")
        dxo = PCtr(es, nc, "dxo")
        dkd = PCtr(es, nc, "dkd")
        dqd = PCtr(es, nc, "dqd")
        dqT = Ctr(es, nc, "dqT")
        dkT = Ctr(es, nc, "dkT")
        dxr = Ctr(es, nc, "dxr")
        dwm = Ctr(es, nc, "dwm")
        dy = PCtr(es, nc, "dy")

        em("g", [], lambda g: g.dma_start(out=ident[:], in_=ident_d[:]), [dmisc.bump(16)])
        em("g", [], lambda g: g.dma_start(out=masks[:], in_=masks_d[:]), [dmisc.bump(16)])
        em("g", [], lambda g: g.dma_start(out=posq[:], in_=posq_d[:]), [dmisc.bump(16)])
        em("g", [], lambda g: g.dma_start(out=poskv[:], in_=poskv_d[:]), [dmisc.bump(16)])
        em("g", [], lambda g: g.dma_start(out=selq[:], in_=selq_d[:]), [dmisc.bump(16)])
        em("g", [], lambda g: g.dma_start(out=b_sb[:], in_=qkv_b[:]), [dwA.bump(16)])
        em("g", [], lambda g: g.dma_start(
            out=wkv[:], in_=qkv_wT[:, 4096:4608].rearrange("(fc p) o -> p fc o", p=128)),
            [dwA.bump(16)])
        em("g", [], lambda g: g.iota(tabsc[:, 0, :], pattern=[[1, 32]], base=0,
                                     channel_multiplier=0,
                                     allow_small_or_imprecise_dtypes=True), [G.bump()])

        em("v", [], lambda v: v.memset(wmsc[:], 0.0))
        em("v", [], lambda v: v.memset(consts[:, 0:1], ESH))
        em("v", [], lambda v: v.memset(consts[:, 1:2], PI / 2.0))
        em("v", [], lambda v: v.memset(consts[:, 2:3], 1e-5))
        em("v", [], lambda v: v.memset(ones[:], 1.0), [V.bump()])
        v_ones = V.n
        em("v", [G.at(1)], lambda v: v.tensor_scalar(
            out=tabsc[:, 1, :], in0=tabsc[:, 0, :], scalar1=-LN10K / 32.0, scalar2=None,
            op0=ALU.mult), [V.bump()])
        em("s", [V.now()], lambda s: s.activation(out=invfb[:], in_=tabsc[:, 1, :],
                                                  func=AF.Exp), [S.bump()])
        s_invf = S.n

        def wide_tables(nset, pos_ap, dst_c, dst_s):
            aA = wtA[:, 0:nset, :]
            aB = wtB[:, 0:nset, :]
            aI = wtI[:, 0:nset, :]
            aF = wtF[:, 0:nset, :]
            em("v", [S.at(s_invf), dmisc.at(80), S.now()],
               lambda v, p=pos_ap, nset=nset, aA=aA: v.tensor_mul(
                   out=aA, in0=p.unsqueeze(2).broadcast_to([128, nset, 32]),
                   in1=invfb.unsqueeze(1).broadcast_to([128, nset, 32])), [V.bump()])
            for off, dst in ((None, dst_s), (0.25, dst_c)):
                if off is None:
                    em("v", [V.now(), S.now()], lambda v, aA=aA, aB=aB: v.tensor_scalar(
                        out=aB, in0=aA, scalar1=I2PI, scalar2=None, op0=ALU.mult),
                        [V.bump()])
                else:
                    em("v", [V.now(), S.now()], lambda v, off=off, aA=aA, aB=aB:
                       v.tensor_scalar(out=aB, in0=aA, scalar1=I2PI, scalar2=off,
                                       op0=ALU.mult, op1=ALU.add), [V.bump()])
                em("v", [V.now()], lambda v, aB=aB, aI=aI: v.tensor_copy(
                    out=aI, in_=aB), [V.bump()])
                em("v", [V.now()], lambda v, aI=aI, aF=aF: v.tensor_copy(
                    out=aF, in_=aI), [V.bump()])
                em("v", [V.now()], lambda v, aB=aB, aF=aF: v.tensor_sub(
                    out=aB, in0=aB, in1=aF), [V.bump()])
                em("v", [V.now()], lambda v, aB=aB, aF=aF: v.tensor_scalar(
                    out=aF, in0=aB, scalar1=0.5, scalar2=None, op0=ALU.is_ge),
                    [V.bump()])
                em("v", [V.now()], lambda v, aB=aB, aF=aF: v.tensor_sub(
                    out=aB, in0=aB, in1=aF), [V.bump()])
                em("v", [V.now()], lambda v, aB=aB: v.tensor_scalar(
                    out=aB, in0=aB, scalar1=2.0 * PI, scalar2=None, op0=ALU.mult),
                    [V.bump()])
                em("s", [V.now()], lambda s, d=dst, aB=aB: s.activation(
                    out=d, in_=aB, func=AF.Sin), [S.bump()])

        wide_tables(8, posq[:], qtab_c[:], qtab_s[:])
        wide_tables(32, poskv[:], kvtab_c[:], kvtab_s[:])
        s_tabs = S.n
        v_wtabs = V.n

        em("t", [dwA.at(32), V.at(v_ones)], lambda t: t.matmul(
            PS[:, 0, :], ones[0:1, :], b_sb[0:1, 4096:4608], start=True, stop=True),
            [T.bump()])
        t_bkv = T.n
        em("v", [T.at(t_bkv)], lambda v: v.tensor_copy(out=bias_kv[:], in_=PS[:, 0, :]),
           [V.bump()])
        v_biaskv = V.n
        for f in range(32):
            em("t", [T.at(t_bkv)] if f == 0 else [], lambda t, f=f: t.matmul(
                PS[:, 7, :], ones[:], wkv[:, f, :], start=(f == 0), stop=(f == 31)),
                [T.bump()] if f == 31 else ())
        em("v", [T.now()], lambda v: v.tensor_copy(out=cskv[:], in_=PS[:, 7, :]),
           [V.bump()])
        v_cskv = V.n

        def rope(dst3, tabc, tabs, nh, wfirst, scratch):
            ev = dst3[:, :, 0:64:2]
            od = dst3[:, :, 1:64:2]
            cb = tabc.unsqueeze(1).broadcast_to([128, nh, 32])
            sbt = tabs.unsqueeze(1).broadcast_to([128, nh, 32])
            rr = [scratch[:, k, 0:32 * nh].rearrange("p (h i) -> p h i", h=nh)
                  for k in range(4)]
            em("v", wfirst, lambda v: v.tensor_mul(out=rr[0], in0=ev, in1=cb))
            em("v", [], lambda v: v.tensor_mul(out=rr[1], in0=od, in1=sbt))
            em("v", [], lambda v: v.tensor_mul(out=rr[2], in0=od, in1=cb))
            em("v", [], lambda v: v.tensor_mul(out=rr[3], in0=ev, in1=sbt), [V.bump()])
            em("v", [V.now()], lambda v: v.tensor_sub(out=ev, in0=rr[0], in1=rr[1]))
            em("v", [], lambda v: v.tensor_add(out=od, in0=rr[2], in1=rr[3]), [V.bump()])

        ckv, ckve, ckvr = [], [], []
        ckv_last = []
        ccast = []
        cast_l = []
        csq_last = []
        crowbf = []
        cevac = {}
        crow_read = []
        tsq = {}
        tmean_stop = {}
        cown = {}

        def emit_stats_tc(tc):
            par = tc % 2
            for l2 in range(4):
                wl = []
                if l2 == 0 and tc >= 2:
                    wl += [T.at(ckv_last[tc - 2]), S.at(csq_last[tc - 2])]
                em("g", wl, lambda q, tc=tc, l2=l2: q.dma_start(
                    out=xnT_blk[:, tc % 2, 8 * l2:8 * (l2 + 1), :],
                    in_=xT[1024 * l2:1024 * (l2 + 1),
                           512 * tc:512 * (tc + 1)].rearrange(
                        "(f p) t -> p f t", p=128)), [dxf[tc].bump(16)])
            d_tc = dxf[tc].n
            ccast.append(d_tc)
            for fc in range(32):
                wt = []
                if fc == 0:
                    wt.append(dxf[tc].at(d_tc))
                    if tc >= 1:
                        wt.append(V.at(crow_read[tc - 1]))
                em("t", wt, lambda t, tc=tc, fc=fc: t.matmul(
                    PS[:, 3, :], ones[:], xnT_blk[:, tc % 2, fc, :],
                    start=(fc == 0), stop=(fc == 31)),
                    [T.bump()] if fc == 31 else ())
                if fc == 31:
                    tmean_stop[tc] = T.n
                ws = []
                if fc == 0:
                    ws.append(dxf[tc].at(d_tc))
                    if tc == 0:
                        ws.append(V.at(v_wtabs))
                if (tc, fc - 8) in tsq:
                    ws.append(T.at(tsq[(tc, fc - 8)]))
                elif tc >= 1 and fc < 8:
                    ws.append(T.at(tsq[(tc - 1, 24 + fc)]))
                em("s", ws, lambda s, tc=tc, fc=fc: s.activation(
                    out=sqb[:, fc % 8, :], in_=xnT_blk[:, tc % 2, fc, :],
                    func=AF.Square), [S.bump()])
                s_here = S.n
                wq4 = [S.at(s_here)]
                if fc == 0 and tc >= 1:
                    wq4.append(V.at(crow_read[tc - 1]))
                em("t", wq4, lambda t, tc=tc, fc=fc: t.matmul(
                    PS[:, 4, :], ones[:], sqb[:, fc % 8, :],
                    start=(fc == 0), stop=(fc == 31)), [T.bump()])
                tsq[(tc, fc)] = T.n
            csq_last.append(S.n)
            wv = [T.at(tmean_stop[tc])]
            if tc >= 2:
                wv.append(V.at(crow_read[tc - 2]))
            em("v", wv, lambda v, par=par: v.tensor_scalar(
                out=mrow[0:1, par, :], in0=PS[0:1, 3, :], scalar1=1.0 / 4096.0,
                scalar2=None, op0=ALU.mult), [V.bump()])
            em("v", [V.now()], lambda v, par=par: v.tensor_mul(
                out=nrrow[0:1, par, :], in0=mrow[0:1, par, :], in1=mrow[0:1, par, :]),
                [V.bump()])
            em("v", [V.now(), T.at(tsq[(tc, 31)])], lambda v, par=par:
               v.scalar_tensor_tensor(
                   out=nrrow[0:1, par, :], in0=PS[0:1, 4, :], scalar=1.0 / 4096.0,
                   in1=nrrow[0:1, par, :], op0=ALU.mult, op1=ALU.subtract), [V.bump()])
            em("s", [V.now()], lambda s, par=par: s.activation(
                out=nrrow[0:1, par, :], in_=nrrow[0:1, par, :], func=AF.Ln,
                bias=consts[0:1, 2:3]), [S.bump()])
            em("s", [S.now()], lambda s, par=par: s.activation(
                out=nrrow[0:1, par, :], in_=nrrow[0:1, par, :], func=AF.Exp,
                scale=-0.5), [S.bump()])
            em("v", [S.now()], lambda v, par=par: v.tensor_scalar(
                out=nrrow[0:1, par, :], in0=nrrow[0:1, par, :], scalar1=-1.0,
                scalar2=None, op0=ALU.mult), [V.bump()])
            em("v", [V.now()], lambda v, par=par: v.tensor_copy(
                out=mrow_bf[0:1, par, :], in_=mrow[0:1, par, :]), [V.bump()])
            em("v", [V.now()], lambda v, par=par: v.tensor_copy(
                out=nrrow_bf[0:1, par, :], in_=nrrow[0:1, par, :]), [V.bump()])
            crowbf.append(V.n)
            for kl in range(4):
                kt = 4 * tc + kl
                wtm = [V.at(crowbf[tc])]
                if kt >= 1:
                    wtm.append(V.at(cevac[kt - 1]))
                if kl == 0 and tc >= 2 and tc % 2 == 0:
                    wtm.append(V.at(cown[((tc - 2) // 2, 6)]))
                em("t", wtm, lambda t, par=par, kl=kl: t.matmul(
                    PS[:, 5, 0:1], mrow_bf[0:1, par, 128 * kl:128 * (kl + 1)],
                    ones[0:1, 0:1], start=True, stop=True), [T.bump()])
                em("t", [], lambda t, par=par, kl=kl: t.matmul(
                    PS[:, 6, 0:1], nrrow_bf[0:1, par, 128 * kl:128 * (kl + 1)],
                    ones[0:1, 0:1], start=True, stop=True), [T.bump()])
                tcol = T.n
                em("v", [T.at(tcol)], lambda v, kt=kt: v.tensor_copy(
                    out=m_all[:, kt:kt + 1], in_=PS[:, 5, 0:1]))
                em("v", [], lambda v, kt=kt: v.tensor_copy(
                    out=m_allb[:, kt:kt + 1], in_=PS[:, 5, 0:1]))
                em("v", [], lambda v, kt=kt: v.tensor_copy(
                    out=nr_all[:, kt:kt + 1], in_=PS[:, 6, 0:1]))
                em("v", [], lambda v, kt=kt: v.tensor_copy(
                    out=nr_allb[:, kt:kt + 1], in_=PS[:, 6, 0:1]), [V.bump()])
                cevac[kt] = V.n
            if tc % 2 == 1:
                t = (tc - 1) // 2
                for bank, src, dst in ((5, m_allb, m_own), (6, nr_allb, nr_own)):
                    for q in range(8):
                        kt = 8 * t + q
                        w = [V.at(cevac[4 * tc + 3]), dmisc.at(80)] if q == 0 else []
                        em("t", w, lambda tt, bank=bank, q=q, kt=kt, src=src: tt.matmul(
                            PS[:, bank, 0:1], selq[:, q, :], src[:, kt:kt + 1],
                            start=(q == 0), stop=(q == 7)),
                            [T.bump()] if q == 7 else ())
                    em("v", [T.now()], lambda v, bank=bank, t=t, dst=dst:
                       v.tensor_copy(out=dst[:, t:t + 1], in_=PS[:, bank, 0:1]),
                       [V.bump()])
                    cown[(t, bank)] = V.n
            crow_read.append(V.n)
            ckv_last.append(0)

        def emit_kv_tc(tc):
            par = tc % 2
            for kl in range(4):
                kt = 4 * tc + kl
                w = [dxf[tc].at(ccast[tc]), dwA.at(32)]
                if kt < 3:
                    w.append(V.at(v_cskv))
                if kt >= 3:
                    w.append(V.at(ckve[kt - 3]))
                for f in range(32):
                    em("t", w if f == 0 else [],
                       lambda t, par=par, kl=kl, kt=kt, f=f: t.matmul(
                           PS[:, kt % 3, :],
                           xnT_blk[:, par, f, 128 * kl:128 * (kl + 1)],
                           wkv[:, f, :], start=(f == 0), stop=(f == 31)),
                       [T.bump()] if f == 31 else ())
                ckv.append(T.n)
                wv = [T.at(ckv[kt]), V.at(cevac[kt])]
                if kt >= 1:
                    wv.append(V.at(ckve[kt - 1]))
                if kt >= 2:
                    wv.append(dkd[kt].at(16 * (kt // 2)))
                em("v", wv, lambda v, kt=kt: v.scalar_tensor_tensor(
                    out=kvt1[:], in0=cskv[:], scalar=m_all[:, kt:kt + 1],
                    in1=PS[:, kt % 3, :], op0=ALU.mult, op1=ALU.subtract), [V.bump()])
                em("v", [V.now()], lambda v, kt=kt: v.scalar_tensor_tensor(
                    out=kvev[:, kt % 2, :], in0=kvt1[:], scalar=nr_all[:, kt:kt + 1],
                    in1=bias_kv[:], op0=ALU.mult, op1=ALU.add), [V.bump()])
                ckve.append(V.n)
                em("v", [S.at(s_tabs), V.at(ckve[kt])], lambda v, kt=kt: v.tensor_copy(
                    out=v_cache[:, kt, :, :],
                    in_=kvev[:, kt % 2, 256:512].rearrange("p (g d) -> p g d", g=2)))
                kview = kvev[:, kt % 2, 0:256].rearrange("p (h d) -> p h d", h=2)
                rope(kview, kvtab_c[:, kt % 32, :], kvtab_s[:, kt % 32, :], 2, [],
                     ropes_kv)
                ckvr.append(V.n)
                em("y", [V.at(ckvr[kt])], lambda yq, kt=kt: yq.dma_start(
                    out=k_d[128 * kt:128 * (kt + 1), :], in_=kvev[:, kt % 2, 0:256]),
                    [dkd[kt].bump(16)])
            ckv_last[tc] = T.n

        for tc in range(16):
            if tc >= 1:
                emit_kv_tc(tc - 1)
            emit_stats_tc(tc)
        emit_kv_tc(15)

        for l2 in range(8):
            wl = [T.at(ckv_last[15]), S.at(csq_last[15])] if l2 == 0 else []
            em("g", wl, lambda q, l2=l2: q.dma_start(
                out=xnT_own[:, 4 * l2:4 * (l2 + 1), :],
                in_=xT_own[512 * l2:512 * (l2 + 1), :].rearrange(
                    "(f p) t -> p f t", p=128)), [dxo[0].bump(16)])
        d_own = dxo[0].n

        for ob in range(8):
            w = [V.at(ckve[63])] if ob == 0 else [V.at(V.n)]
            em("t", w, lambda t, ob=ob: t.matmul(PS[:, 5, :], ones[0:1, :],
                                                 b_sb[0:1, 512 * ob:512 * (ob + 1)],
                                                 start=True, stop=True), [T.bump()])
            em("v", [T.now()], lambda v, ob=ob: v.tensor_copy(
                out=bias_q[:, 512 * ob:512 * (ob + 1)], in_=PS[:, 5, :]), [V.bump()])
        v_qb = V.n
        t_qb2 = T.n

        em("g", [V.at(ckvr[63]), T.at(ckv_last[15])] + dkd.both_at(32 * 16),
           lambda g: g.dma_start(
            out=awT_sb[:], in_=aw_T.rearrange("(fc p) o -> p fc o", p=128)), [dwA.bump(16)])
        dwA_awt = dwA.n

        cqm, cqe, cqr = [], [], []
        ccsq = []
        for ob in range(8):
            wg = [T.at(ckv_last[15])]
            if ob < 2:
                wg += [T.at(t_qb2), V.at(ckvr[63])] + dkd.both_at(32 * 16)
            if ob >= 2:
                wg.append(T.at(cqm[8 * (ob - 2) + 7]))
            em("g", wg, lambda g, ob=ob: g.dma_start(
                out=wq[:, ob % 2, :, :],
                in_=qkv_wT[:, 512 * ob:512 * (ob + 1)].rearrange("(fc p) o -> p fc o", p=128)),
                [dwq[ob].bump(16)])
            d_wq = dwq[ob].n
            for f in range(32):
                w = [dwq[ob].at(d_wq)] if f == 0 else []
                if f == 0 and ob >= 1:
                    w.append(V.at(ccsq[ob - 1]))
                if f == 0 and ob == 0:
                    w.append(V.at(v_qb))
                em("t", w, lambda t, ob=ob, f=f: t.matmul(
                    PS[:, 4, :], ones[:], wq[:, ob % 2, f, :],
                    start=(f == 0), stop=(f == 31)), [T.bump()] if f == 31 else ())
            em("v", [T.now()] + ([V.at(ckve[63])] if ob == 0 else []),
               lambda v, ob=ob: v.tensor_copy(
                out=csq_sb[:, ob % 2, :], in_=PS[:, 4, :]), [V.bump()])
            ccsq.append(V.n)
            for t in range(8):
                n = 8 * ob + t
                w = [dwq[ob].at(d_wq), dxo[0].at(d_own), V.at(ccsq[ob])]
                if ob == 0 and t == 0:
                    w.append(V.at(ckve[63]))
                if t == 0:
                    w += [V.at(cown[(0, 5)]), V.at(cown[(7, 6)])]
                if n >= 4:
                    w.append(V.at(cqe[n - 4]))
                for f in range(32):
                    em("t", w if f == 0 else [],
                       lambda tt, ob=ob, t=t, f=f: tt.matmul(
                           PS[:, t % 4, :], xnT_own[:, f, 128 * t:128 * (t + 1)],
                           wq[:, ob % 2, f, :], start=(f == 0), stop=(f == 31)),
                       [T.bump()] if f == 31 else ())
                cqm.append(T.n)
                wv = [T.at(cqm[n])]
                if n == 0:
                    wv.append(V.at(ckve[63]))
                if n >= 1:
                    wv.append(V.at(cqe[n - 1]))
                if n >= 2:
                    wv.append(dqd[n].at(16 * (n // 2)))
                em("v", wv, lambda v, n=n, t=t, ob=ob: v.scalar_tensor_tensor(
                    out=kvt1[:], in0=csq_sb[:, ob % 2, :], scalar=m_own[:, t:t + 1],
                    in1=PS[:, t % 4, :], op0=ALU.mult, op1=ALU.subtract), [V.bump()])
                em("v", [V.now()], lambda v, n=n, t=t, ob=ob: v.scalar_tensor_tensor(
                    out=qev[:, n % 2, :], in0=kvt1[:], scalar=nr_own[:, t:t + 1],
                    in1=bias_q[:, 512 * ob:512 * (ob + 1)], op0=ALU.mult, op1=ALU.add),
                    [V.bump()])
                cqe.append(V.n)
                qview = qev[:, n % 2, :].rearrange("p (h d) -> p h d", h=4)
                rope(qview, qtab_c[:, t, :], qtab_s[:, t, :], 4, [V.at(cqe[n])],
                     ropes_q)
                cqr.append(V.n)
                em("y", [V.at(cqr[n])], lambda yq, n=n, t=t, ob=ob: yq.dma_start(
                    out=q_d[128 * t:128 * (t + 1), 512 * ob:512 * (ob + 1)],
                    in_=qev[:, n % 2, :]), [dqd[n].bump(16)])

        for f in range(32):
            w = [dwA.at(dwA_awt), V.at(ccsq[7])] if f == 0 else []
            em("t", w, lambda t, f=f: t.matmul(
                PS[:, 4, 0:68], ones[:], awT_sb[:, f, :],
                start=(f == 0), stop=(f == 31)), [T.bump()] if f == 31 else ())
        em("v", [T.now(), V.at(ckve[63])], lambda v: v.tensor_copy(
            out=csA_sb[:], in_=PS[:, 4, 0:68]), [V.bump()])
        v_csA = V.n
        cme = []
        t_mid_last = 0
        for t in range(8):
            w = [dwA.at(dwA_awt), V.at(cqe[63])]
            if t >= 2:
                w.append(V.at(cme[t - 2]))
            for f in range(32):
                em("t", w if f == 0 else [],
                   lambda tt, t=t, f=f: tt.matmul(
                       PS[:, t % 2, 0:68], xnT_own[:, f, 128 * t:128 * (t + 1)],
                       awT_sb[:, f, :], start=(f == 0), stop=(f == 31)),
                   [T.bump()] if f == 31 else ())
            t_mid_last = T.n
            wmf = [T.at(t_mid_last), V.at(v_csA)]
            if t >= 1:
                wmf.append(V.at(cme[t - 1]))
            em("v", wmf, lambda v, t=t: v.scalar_tensor_tensor(
                out=mid_f[:, 0:68], in0=csA_sb[:], scalar=m_own[:, t:t + 1],
                in1=PS[:, t % 2, 0:68], op0=ALU.mult, op1=ALU.subtract), [V.bump()])
            em("v", [V.now()], lambda v, t=t: v.tensor_scalar(
                out=mid_f[:, 0:68], in0=mid_f[:, 0:68], scalar1=nr_own[:, t:t + 1],
                scalar2=None, op0=ALU.mult), [V.bump()])
            L = lambda j: mid_f[:, 64 + j:65 + j]
            em("v", [V.now()], lambda v: v.tensor_max(out=stat[:, 0:1], in0=L(0), in1=L(1)))
            em("v", [], lambda v: v.tensor_max(out=stat[:, 1:2], in0=L(2), in1=L(3)))
            em("v", [], lambda v: v.tensor_tensor(out=stat[:, 2:3], in0=L(0), in1=L(1),
                                                  op=ALU.min))
            em("v", [], lambda v: v.tensor_tensor(out=stat[:, 3:4], in0=L(2), in1=L(3),
                                                  op=ALU.min), [V.bump()])
            em("v", [V.now()], lambda v: v.tensor_tensor(out=stat[:, 4:5], in0=stat[:, 0:1],
                                                         in1=stat[:, 1:2], op=ALU.min))
            em("v", [], lambda v: v.tensor_max(out=stat[:, 5:6], in0=stat[:, 2:3],
                                               in1=stat[:, 3:4]))
            em("v", [], lambda v: v.tensor_max(out=stat[:, 6:7], in0=stat[:, 0:1],
                                               in1=stat[:, 1:2]), [V.bump()])
            em("v", [V.now()], lambda v: v.tensor_max(out=stat[:, 4:5], in0=stat[:, 4:5],
                                                      in1=stat[:, 5:6]))
            em("v", [], lambda v: v.tensor_scalar(out=stat[:, 7:8], in0=stat[:, 6:7],
                                                  scalar1=-1.0, scalar2=None,
                                                  op0=ALU.mult), [V.bump()])
            em("s", [V.now()], lambda s: s.activation(
                out=mid_f[:, 72:76], in_=mid_f[:, 64:68], func=AF.Exp, bias=stat[:, 7:8]),
                [S.bump()])
            em("v", [S.now()], lambda v: v.tensor_scalar(
                out=mid_f[:, 76:80], in0=mid_f[:, 64:68], scalar1=stat[:, 4:5],
                scalar2=None, op0=ALU.is_ge), [V.bump()])
            em("v", [V.now()], lambda v: v.tensor_mul(
                out=mid_f[:, 72:76], in0=mid_f[:, 72:76], in1=mid_f[:, 76:80]), [V.bump()])
            em("v", [V.now()], lambda v: v.reduce_sum(
                out=stat[:, 0:1], in_=mid_f[:, 72:76], axis=AX.X), [V.bump()])
            em("v", [V.now()], lambda v: v.reciprocal(out=stat[:, 1:2], in_=stat[:, 0:1]),
               [V.bump()])
            em("v", [V.now()], lambda v, t=t: v.tensor_scalar(
                out=gates[:, t, :], in0=mid_f[:, 72:76], scalar1=stat[:, 1:2],
                scalar2=4.0, op0=ALU.mult, op1=ALU.mult), [V.bump()])
            for e in range(4):
                w = [V.now()]
                if e == 0 and t >= 1:
                    w.append(dwm.at(16 * t))
                em("v", w if e == 0 else [], lambda v, t=t, e=e: v.tensor_scalar(
                    out=wmsc[:, 16 * e:16 * (e + 1)], in0=mid_f[:, 16 * e:16 * (e + 1)],
                    scalar1=gates[:, t, e:e + 1], scalar2=None, op0=ALU.mult), [V.bump()])
            cme.append(V.n)
            v_wm = V.n
            em("g", [V.at(v_wm)], lambda g, t=t: g.dma_start(
                out=wm2_d[128 * t:128 * (t + 1), :], in_=wmsc[:]), [dwm.bump(16)])

        for g2 in range(2):
            for ss in range(8):
                w = (dkd.both_at(32 * 16) + [V.at(cqr[63]), T.at(t_mid_last)]
                     ) if (g2 == 0 and ss == 0) else []
                em("s", w, lambda q, g2=g2, ss=ss: q.dma_start(
                    out=kT[:, g2, 1024 * ss:1024 * (ss + 1)],
                    in_=k_d[1024 * ss:1024 * (ss + 1), 128 * g2:128 * (g2 + 1)],
                    transpose=True), [dkT.bump(16)])
        for h in range(32):
            w = dqd.both_at(32 * 16) + [T.at(t_mid_last)] if h == 0 else []
            em("s", w, lambda q, h=h: q.dma_start(
                out=qT[:, h, :], in_=q_d[:, 128 * h:128 * (h + 1)], transpose=True),
                [dqT.bump(16)])

        meta = []
        gidx = 0
        for qt in range(8):
            wwin, bb = qt % 4, qt // 4
            for g2 in range(2):
                for h2 in range(2):
                    hh0 = 16 * g2 + 8 * h2
                    nch = 8 * (wwin + 1)
                    for j in range(nch):
                        meta.append(dict(qt=qt, wwin=wwin, g2=g2, hh0=hh0, j=j,
                                         nch=nch, kt=32 * bb + j, gidx=gidx))
                    gidx += 1
        n_groups = gidx
        NGJ = len(meta)
        cex, csc, cat, cready = [], [], [], []
        cse, cve = [], []

        def emit_scores(GJ):
            m = meta[GJ]
            bs = 2 * (GJ % 2)
            w = []
            if GJ == 0:
                w += [dqT.at(32 * 16), dkT.at(16 * 16), V.at(cme[7]), S.at(s_tabs)]
            if GJ >= 2:
                w.append(S.at(cex[GJ - 2]))
            for half in range(2):
                em("t", w if half == 0 else [],
                   lambda t, g2=m["g2"], kt=m["kt"], qt=m["qt"], hh0=m["hh0"],
                   half=half, bs=bs:
                   t.matmul(PS[:, bs + half, :],
                            kT[:, g2, 128 * kt:128 * (kt + 1)],
                            qT[:, hh0 + 4 * half:hh0 + 4 * (half + 1),
                               128 * qt:128 * (qt + 1)],
                            start=True, stop=True),
                   [T.bump()] if half == 1 else ())
            csc.append(T.n)
            we = [T.at(csc[GJ])]
            if GJ == 0:
                we.append(V.at(cqe[63]))
            if GJ >= 3:
                we.append(T.at(cat[GJ - 3]))
            em("s", we, lambda s, GJ=GJ, bs=bs: s.activation(
                out=probs[:, GJ % 3, :],
                in_=PS[:, bs:bs + 2, :].rearrange("p b n -> p (b n)"),
                func=AF.Exp, scale=ISQ, bias=consts[:, 0:1]), [S.bump()])
            cex.append(S.n)
            if m["j"] >= 8 * m["wwin"]:
                wrel = m["j"] - 8 * m["wwin"]
                em("v", [S.at(cex[GJ])], lambda v, GJ=GJ, wrel=wrel:
                   v.tensor_mul(
                       out=probs[:, GJ % 3, :].rearrange("p (h q) -> p h q", h=8),
                       in0=probs[:, GJ % 3, :].rearrange("p (h q) -> p h q", h=8),
                       in1=masks[:, wrel, :].unsqueeze(1).broadcast_to(
                           [128, 8, 128])), [V.bump()])
                cready.append((V.h, V.n))
            else:
                cready.append((S.h, cex[GJ]))

        def emit_sumctx(GJ):
            m = meta[GJ]
            wsum = [cready[GJ]]
            if m["j"] == 0 and m["gidx"] >= 1:
                wsum += [S.at(cse[m["gidx"] - 1]), V.at(cve[m["gidx"] - 1])]
            first, last = (m["j"] == 0), (m["j"] == m["nch"] - 1)
            for half in range(2):
                em("t", wsum if half == 0 else [],
                   lambda t, GJ=GJ, half=half, first=first, last=last:
                   t.matmul(PS[:, 4 + half, :], ones[:],
                            probs[:, GJ % 3, 512 * half:512 * (half + 1)],
                            start=first, stop=last))
            for half in range(2):
                em("t", [], lambda t, GJ=GJ, half=half, kt=m["kt"], g2=m["g2"],
                   first=first, last=last:
                   t.matmul(PS[:, 6 + half, :], v_cache[:, kt, g2, :],
                            probs[:, GJ % 3, 512 * half:512 * (half + 1)],
                            start=first, stop=last),
                   [T.bump()] if half == 1 else ())
            cat.append(T.n)
            if last:
                g = m["gidx"]
                we = [T.at(cat[GJ])]
                if g == 0:
                    we += dqd.both_at(32 * 16)
                em("s", we, lambda s: s.activation(
                    out=rbc[:], in_=PS[:, 4:6, :].rearrange("p b n -> p (b n)"),
                    func=AF.Ln), [S.bump()])
                em("s", [S.now()], lambda s: s.activation(
                    out=rbc[:], in_=rbc[:], func=AF.Exp, scale=-1.0), [S.bump()])
                cse.append(S.n)
                em("v", [S.at(cse[g]), T.at(cat[GJ])],
                   lambda v, hh0=m["hh0"], qt=m["qt"]: v.tensor_mul(
                       out=ctxT[:, hh0:hh0 + 8, qt, :],
                       in0=PS[:, 6:8, :].rearrange("p b n -> p (b n)").rearrange(
                           "p (h q) -> p h q", h=8),
                       in1=rbc[:].rearrange("p (h q) -> p h q", h=8)), [V.bump()])
                cve.append(V.n)

        for GJ in range(NGJ + 1):
            if GJ < NGJ:
                emit_scores(GJ)
            if GJ >= 1:
                emit_sumctx(GJ - 1)
        t_att_end = T.n
        s_att_end = S.n

        em("g", [T.at(t_att_end)], lambda g: g.dma_start(
            out=bT_sb[:], in_=bTd[:]), [dbt.bump(16)])
        em("s", [dwm.at(8 * 16), T.at(t_att_end), V.at(cve[n_groups - 1])],
           lambda q: q.dma_start(out=wmT_sb[:], in_=wm2_d[:], transpose=True),
           [dwmT.bump(16)])
        cds, cdd, csl, csg = [], [], [], []
        for od in range(16):
            w = [T.at(t_att_end)]
            if od >= 2:
                w.append(T.at(cds[8 * (od - 2) + 7]))
            em("g", w, lambda g, od=od: g.dma_start(
                out=wd[:, od % 2, :, :],
                in_=dense_wT[:, 256 * od:256 * (od + 1)].rearrange(
                    "(fc p) o -> p fc o", p=128)), [dwd[od].bump(16)])
            d_wd = dwd[od].n
            for t in range(8):
                it = 8 * od + t
                w = [dwd[od].at(d_wd), V.at(cve[n_groups - 1]), S.at(cex[GJ - 1])]
                if it >= 2:
                    w.append(V.at(csg[it - 2]))
                for h in range(32):
                    em("t", w if h == 0 else [],
                       lambda tt, od=od, t=t, h=h, it=it: tt.matmul(
                           PS[:, it % 2, 0:256], ctxT[:, h, t, :], wd[:, od % 2, h, :],
                           start=(h == 0), stop=(h == 31)), [T.bump()] if h == 31 else ())
                cds.append(T.n)
                wdd = [dbt.at(16), dwmT.at(16), S.at(cex[GJ - 1])]
                if it >= 2:
                    wdd.append(S.at(csl[it - 2]))
                em("t", wdd, lambda tt, od=od, t=t, it=it: tt.matmul(
                    PS[:, 2 + it % 2, 0:256], wmT_sb[0:64, 128 * t:128 * (t + 1)],
                    bT_sb[0:64, 256 * od:256 * (od + 1)],
                    start=True, stop=True), [T.bump()])
                cdd.append(T.n)
                wsl = [T.at(cdd[it])]
                if it >= 1:
                    wsl.append(V.at(csg[it - 1]))
                em("s", wsl, lambda s, it=it: s.activation(
                    out=gsc[:], in_=PS[:, 2 + it % 2, 0:256], func=AF.Silu),
                    [S.bump()])
                csl.append(S.n)
                wsg = [T.at(cds[it]), S.at(csl[it])]
                em("v", wsg, lambda v, od=od, t=t, it=it: v.tensor_mul(
                    out=sg[:, t, 256 * od:256 * (od + 1)], in0=PS[:, it % 2, 0:256],
                    in1=gsc[:]), [V.bump()])
                csg.append(V.n)

        cfin = []
        for t in range(8):
            w = [V.at(csg[127]), T.at(cds[127])]
            if t >= 1:
                w.append(V.at(cfin[8 * t - 1]))
            em("y", w, lambda yq, t=t: yq.dma_start(
                out=xres[:], in_=x_own[128 * t:128 * (t + 1), :]), [dxr.bump(16)])
            d_x = dxr.n
            for ch in range(8):
                n = 8 * t + ch
                wv = [dxr.at(d_x)]
                if n >= 1:
                    wv.append(V.at(cfin[n - 1]))
                if n >= 2:
                    wv.append(dy[n].at(16 * (n // 2)))
                em("v", wv, lambda v, t=t, ch=ch: v.tensor_copy(
                    out=sgf[:], in_=sg[:, t, 512 * ch:512 * (ch + 1)]), [V.bump()])
                em("v", [V.now()], lambda v, n=n, ch=ch: v.tensor_add(
                    out=outb[:, n % 2, :], in0=sgf[:],
                    in1=xres[:, 512 * ch:512 * (ch + 1)]), [V.bump()])
                cfin.append(V.n)
                em("y", [V.at(cfin[n])], lambda yq, t=t, ch=ch, n=n: yq.dma_start(
                    out=y[128 * t:128 * (t + 1), 512 * ch:512 * (ch + 1)],
                    in_=outb[:, n % 2, :]), [dy[n].bump(16)])
        em("y", [], lambda yq: yq.wait_ge(dy[0].h, 32 * 16))
        em("y", [], lambda yq: yq.wait_ge(dy[1].h, 32 * 16))

        with nc.Block() as block:
            @block.sync
            def _(sync):
                for fn in OPS["y"]:
                    fn(sync)

            @block.scalar
            def _(scalar):
                for fn in OPS["s"]:
                    fn(scalar)

            @block.vector
            def _(vector):
                for fn in OPS["v"]:
                    fn(vector)

            @block.tensor
            def _(tensor):
                for fn in OPS["t"]:
                    fn(tensor)

            @block.gpsimd
            def _(gpsimd):
                for fn in OPS["g"]:
                    fn(gpsimd)
    return nc


def kernel(L0_attn_hidden_states, L0_attn_qkv_w, L0_attn_qkv_b, L0_attn_dense_w,
           L0_ffn_router_w, L0_ffn_A, L0_ffn_B):
    x = np.ascontiguousarray(np.asarray(L0_attn_hidden_states, np.float32).reshape(8192, 4096))
    qkv_wT = np.ascontiguousarray(np.asarray(L0_attn_qkv_w, np.float32).T)
    dense_wT = np.ascontiguousarray(np.asarray(L0_attn_dense_w, np.float32).T)
    Aw = np.asarray(L0_ffn_A, np.float32)
    aw_T = np.ascontiguousarray(np.concatenate(
        [Aw.transpose(2, 0, 1).reshape(4096, 64),
         np.asarray(L0_ffn_router_w, np.float32).T], axis=1))
    bT = np.ascontiguousarray(
        np.asarray(L0_ffn_B, np.float32).transpose(0, 2, 1).reshape(64, 4096))
    qkv_b = np.ascontiguousarray(np.asarray(L0_attn_qkv_b, np.float32).reshape(1, 4608))
    ident = np.eye(128, dtype=ml_dtypes.bfloat16)
    jj = np.arange(128)
    poskv = (np.arange(32)[None, :] * 128 + jj[:, None]).astype(np.float32)
    xT = np.ascontiguousarray(x.T)

    in_maps = []
    for c in range(8):
        masks = np.zeros((128, 8, 128), dtype=ml_dtypes.bfloat16)
        for w in range(8):
            masks[:, w, :] = ((128 * w + jj[:, None]) <= (8 * jj[None, :] + c)).astype(
                ml_dtypes.bfloat16)
        posq = np.zeros((128, 8), np.float32)
        for qt in range(8):
            posq[:, qt] = 1024 * (qt % 4) + 8 * jj + c
        selq = np.zeros((128, 8, 128), dtype=ml_dtypes.bfloat16)
        for q in range(8):
            tr = 8 * jj + c - 128 * q
            ok = (tr >= 0) & (tr < 128)
            selq[tr[ok], q, jj[ok]] = 1
        in_maps.append({
            "xT": xT, "xT_own": np.ascontiguousarray(xT[:, c::8]),
            "x_own": np.ascontiguousarray(x[c::8, :]),
            "qkv_wT": qkv_wT, "dense_wT": dense_wT, "aw_T": aw_T, "bT": bT,
            "qkv_b": qkv_b, "ident": ident, "masks": np.ascontiguousarray(masks),
            "posq": posq, "poskv": poskv, "selq": selq,
        })

    nc = build()
    res = run_bass_kernel_spmd(nc, in_maps, core_ids=list(range(8)))
    out = np.empty((8192, 4096), np.float32)
    for c in range(8):
        out[c::8, :] = res.results[c]["y"]
    return out.reshape(2, 4096, 4096)
